# version 21
# baseline (speedup 1.0000x reference)
import numpy as np
import ml_dtypes
from contextlib import ExitStack

import concourse.bass as bass
import concourse.mybir as mybir
from concourse.bass_utils import run_bass_kernel_spmd

F32 = mybir.dt.float32
BF16 = mybir.dt.bfloat16
AF = mybir.ActivationFunctionType
ALU = mybir.AluOpType

T = 2048
D = 2048
II = 8192
TC = 512
NTC = T // TC
NDT = D // 128
NIT = II // 64
CH = 256
NCH = T // CH
FTC = 256
NFT = T // FTC
SCALE = float(128 ** -0.5)
R2048 = float(1.0 / np.sqrt(2048.0))
R8192 = float(1.0 / np.sqrt(8192.0))
BF = ml_dtypes.bfloat16


def _had(n):
    h = np.array([[1.0]], dtype=np.float32)
    while h.shape[0] < n:
        h = np.block([[h, h], [h, -h]])
    return h


class Ctr:
    def __init__(self, ctx, nc, name):
        self.h = ctx.enter_context(nc.semaphore(name))
        self.n = 0

    def inc(self, instr, k=1):
        instr.then_inc(self.h, k)
        self.n += k
        return self.n


def dwait(e, sems, r, g):
    if r >= 0:
        e.wait_ge(sems[r % 2].h, 16 * g * (r // 2 + 1))


def build_program(debug=False, upto="G"):
    _ORD = ["0", "A", "B", "C", "Z", "E", "F", "G"]
    _lim = _ORD.index(upto)
    on = lambda ph: _ORD.index(ph) <= _lim
    nc = bass.Bass("TRN2", target_bir_lowering=False)

    x_d = nc.dram_tensor("x", [T, D], F32, kind="ExternalInput").ap()
    wq_d = nc.dram_tensor("wq", [D, D], F32, kind="ExternalInput").ap()
    wk_d = nc.dram_tensor("wk", [D, D], F32, kind="ExternalInput").ap()
    wv_d = nc.dram_tensor("wv", [D, D], F32, kind="ExternalInput").ap()
    wo_d = nc.dram_tensor("wo", [D, D], F32, kind="ExternalInput").ap()
    bq_d = nc.dram_tensor("bq", [D], F32, kind="ExternalInput").ap()
    bk_d = nc.dram_tensor("bk", [D], F32, kind="ExternalInput").ap()
    bv_d = nc.dram_tensor("bv", [D], F32, kind="ExternalInput").ap()
    bo_d = nc.dram_tensor("bo", [D], F32, kind="ExternalInput").ap()
    suu_d = nc.dram_tensor("su_ug", [D], F32, kind="ExternalInput").ap()
    sud_d = nc.dram_tensor("su_dp", [II], F32, kind="ExternalInput").ap()
    wscu_d = nc.dram_tensor("wsc_ug", [2 * II], F32, kind="ExternalInput").ap()
    wscd_d = nc.dram_tensor("wsc_dp", [D], F32, kind="ExternalInput").ap()
    wup_d = nc.dram_tensor("wup", [II, D], F32, kind="ExternalInput").ap()
    wg_d = nc.dram_tensor("wg", [II, D], F32, kind="ExternalInput").ap()
    wd_d = nc.dram_tensor("wd", [D, II], F32, kind="ExternalInput").ap()
    h128_d = nc.dram_tensor("h128", [128, 128], BF16, kind="ExternalInput").ap()
    s2_d = nc.dram_tensor("s2c", [128, 128], BF16, kind="ExternalInput").ap()
    ident_d = nc.dram_tensor("identc", [128, 128], BF16, kind="ExternalInput").ap()
    identf_d = nc.dram_tensor("identf", [128, 128], F32, kind="ExternalInput").ap()

    out_d = nc.dram_tensor("out", [T, D], F32, kind="ExternalOutput").ap()

    scratch_kind = "ExternalOutput" if debug else "Internal"
    ctx2_d = nc.dram_tensor("ctx2", [D, T], BF16, kind=scratch_kind).ap()
    s2d_d = nc.dram_tensor("s2d", [D, T], BF16, kind=scratch_kind).ap()
    y2_d = nc.dram_tensor("y2", [II, T], BF16, kind=scratch_kind).ap()
    gd_d = nc.dram_tensor("gd", [D, T], BF16, kind=scratch_kind).ap()
    z2_d = nc.dram_tensor("z2", [128, NDT * T], BF16, kind=scratch_kind).ap()
    if debug:
        dbg_xnT_d = nc.dram_tensor("dbg_xnT", [128, NDT * T], BF16, kind="ExternalOutput").ap()
        dbg_z_d = nc.dram_tensor("dbg_z", [128, NDT * T], BF16, kind="ExternalOutput").ap()

    gctx = ExitStack()
    with gctx:
        sb = lambda *a: gctx.enter_context(nc.sbuf_tensor(*a))
        sem = lambda name: Ctr(gctx, nc, name)

        h128b = sb("h128b", [128, 128], BF16)
        s2b = sb("s2b", [128, 128], BF16)
        identb = sb("identb", [128, 128], BF16)
        identf = sb("identfb", [128, 128], F32)
        h128su = sb("h128su", [128, NDT * 128], BF16)
        usc = sb("usc", [128, 64], F32)
        gsc = sb("gsc", [128, 64], F32)
        dsc = sb("dsc", [128, 16], F32)
        sucol = sb("sucol", [128, 16], F32)
        nbq = sb("nbq", [128, 16], F32)
        nbk = sb("nbk", [128, 16], F32)
        bqc = sb("bqc", [128, 16], F32)
        bkc = sb("bkc", [128, 16], F32)
        bvs = sb("bvs", [128, 16], F32)
        boc = sb("boc", [128, 16], F32)

        s_ld0 = sem("ld0"); s_ldg = sem("ldg"); s_p0a = sem("p0a"); s_p0b = sem("p0b")
        s_ldx = [sem("ldx0"), sem("ldx1"), sem("ldx2"), sem("ldx3")]; s_stats = sem("stats"); s_aggr = sem("aggr")
        s_vpe = sem("vpe"); s_sd = sem("sd"); s_rs = sem("rs"); s_nm = sem("nm")
        s_xn = sem("xn"); s_trF = sem("trF"); s_trE = sem("trE")
        s_wsl = [sem("wsl0"), sem("wsl1")]; s_wtrF = sem("wtrF"); s_wtrE = sem("wtrE")
        s_mmq = sem("mmq"); s_mmk = sem("mmk"); s_mmv = sem("mmv")
        s_aq = sem("aq"); s_bq = sem("bq"); s_ak = sem("ak"); s_bk = sem("bk")
        s_vs = sem("vs"); s_tq = sem("tq"); s_phiq = sem("phiq"); s_tk = sem("tk")
        s_phik = sem("phik"); s_pv = sem("pv"); s_kv = sem("kv")
        s_ctx = sem("ctx"); s_ctxd = [sem("ctxd0"), sem("ctxd1")]
        s_wosl = [sem("wosl0"), sem("wosl1")]; s_wotrF = sem("wotrF"); s_wotrE = sem("wotrE")
        s_ctxsl = [sem("ctxsl0"), sem("ctxsl1")]; s_mmo = sem("mmo"); s_sev = sem("sev")
        s_sd2 = [sem("sd20"), sem("sd21")]
        s_zmm = sem("zmm"); s_zev = sem("zev"); s_zd2 = [sem("zd20"), sem("zd21")]; s_zld = sem("zld")
        s_st1 = sem("st1"); s_st2 = sem("st2"); s_st3 = sem("st3"); s_st4 = sem("st4")
        s_ewsl = [sem("ewsl0"), sem("ewsl1")]; s_ewtrF = sem("ewtrF"); s_ewtrE = sem("ewtrE")
        s_emu = sem("emu"); s_emg = sem("emg"); s_emy = sem("emy")
        s_esg = sem("esg"); s_eu = sem("eu"); s_eyev = sem("eyev")
        s_eyd = [sem("eyd0"), sem("eyd1")]
        s_fwsl = [sem("fwsl0"), sem("fwsl1")]; s_ftrF = sem("ftrF"); s_ftrE = sem("ftrE")
        s_fs2m = sem("fs2m"); s_fs2e = sem("fs2e")
        s_fysl = [sem("fysl0"), sem("fysl1")]; s_fmm = sem("fmm"); s_fgev = sem("fgev")
        s_fgd = [sem("fgd0"), sem("fgd1")]
        s_gsl = [sem("gsl0"), sem("gsl1")]; s_gsl2 = [sem("gsl20"), sem("gsl21")]; s_grm = sem("grm"); s_gtrF = sem("gtrF"); s_gtrE = sem("gtrE")
        s_gx = [sem("gx0"), sem("gx1")]; s_gadd = sem("gadd")
        s_gout = [sem("gout0"), sem("gout1")]

        with ExitStack() as p0:
            psb = lambda *a: p0.enter_context(nc.sbuf_tensor(*a))
            pps = lambda *a: p0.enter_context(nc.psum_tensor(*a))
            su16 = psb("su16", [16, 128], BF16)
            b16 = psb("b16", [16, 4 * 128], F32)
            sud16 = psb("sud16", [128, 64], F32)
            wsdp16 = psb("wsdp16", [16, 128], F32)
            ps_tr0 = pps("ps_tr0", [128, 16], F32)
            ps_trb = pps("ps_trb", [128, 64], F32)
            ps_tr1 = pps("ps_tr1", [128, 16], BF16)
            sucol_b = psb("sucol_b", [128, 16], BF16)
            blk = p0.enter_context(nc.Block("p0"))

            @blk.sync
            def _(e):
                if not on("0"): return
                e.dma_start(h128b[:], h128_d).then_inc(s_ld0.h, 16)
                e.dma_start(s2b[:], s2_d).then_inc(s_ld0.h, 16)
                e.dma_start(identb[:], ident_d).then_inc(s_ld0.h, 16)
                e.dma_start(identf[:], identf_d).then_inc(s_ld0.h, 16)
                e.dma_start(usc[:], wscu_d.rearrange("(h c j) -> h c j", h=2, j=64)[0]).then_inc(s_ld0.h, 16)
                e.dma_start(gsc[:], wscu_d.rearrange("(h c j) -> h c j", h=2, j=64)[1]).then_inc(s_ld0.h, 16)
                e.dma_start(sud16[:], sud_d.rearrange("(c j) -> c j", j=64)).then_inc(s_ld0.h, 16)
                e.dma_start(wsdp16[:], wscd_d.rearrange("(a p) -> a p", p=128)).then_inc(s_ld0.h, 16)
                for i, b in enumerate([bq_d, bk_d, bv_d, bo_d]):
                    e.dma_start(b16[0:16, i * 128:(i + 1) * 128],
                                b.rearrange("(a p) -> a p", p=128)).then_inc(s_ld0.h, 16)
                s_ld0.n += 16 * 12

            @blk.gpsimd
            def _(e):
                if not on("0"): return
                e.dma_start(su16[:], suu_d.rearrange("(a p) -> a p", p=128)).then_inc(s_ldg.h, 16)
                s_ldg.n += 16

            @blk.tensor
            def _(e):
                if not on("0"): return
                e.wait_ge(s_ld0.h, s_ld0.n)
                e.wait_ge(s_ldg.h, s_ldg.n)
                e.transpose(ps_tr1[:], su16[:], identb[0:16, 0:16]).then_inc(s_p0b.h, 1)
                e.transpose(ps_tr0[:], wsdp16[:], identf[0:16, 0:16]).then_inc(s_p0b.h, 1)
                for i in range(4):
                    e.transpose(ps_trb[:, i * 16:(i + 1) * 16], b16[0:16, i * 128:(i + 1) * 128],
                                identf[0:16, 0:16]).then_inc(s_p0b.h, 1)
                s_p0b.n += 6

            @blk.scalar
            def _(e):
                if not on("0"): return
                e.wait_ge(s_p0b.h, 6)
                e.copy(sucol_b[:], ps_tr1[:]).then_inc(s_p0a.h, 1)
                s_p0a.n += 1
                e.activation(dsc[:], ps_tr0[:], AF.Copy, bias=0.0, scale=R8192).then_inc(s_p0a.h, 1)
                s_p0a.n += 1
                e.copy(bqc[:], ps_trb[:, 0:16]).then_inc(s_p0a.h, 1)
                e.copy(bkc[:], ps_trb[:, 16:32]).then_inc(s_p0a.h, 1)
                e.activation(bvs[:], ps_trb[:, 32:48], AF.Copy, bias=0.0, scale=SCALE).then_inc(s_p0a.h, 1)
                e.copy(boc[:], ps_trb[:, 48:64]).then_inc(s_p0a.h, 1)
                s_p0a.n += 4

            @blk.vector
            def _(e):
                if not on("0"): return
                e.wait_ge(s_p0a.h, s_p0a.n)
                e.wait_ge(s_ld0.h, s_ld0.n)
                e.wait_ge(s_ldg.h, s_ldg.n)
                s_p0a.inc(e.tensor_copy(sucol[:], sucol_b[:]))
                e.wait_ge(s_p0a.h, s_p0a.n)
                e.tensor_tensor(usc[:], usc[:], sud16[:], op=ALU.mult).then_inc(s_p0b.h, 1)
                s_p0b.n += 1
                e.wait_ge(s_p0b.h, s_p0b.n)
                e.tensor_scalar(usc[:], usc[:], R2048, None, op0=ALU.mult).then_inc(s_p0b.h, 1)
                s_p0b.n += 1
                e.tensor_scalar(gsc[:], gsc[:], R2048, None, op0=ALU.mult).then_inc(s_p0b.h, 1)
                s_p0b.n += 1
                e.tensor_scalar(nbq[:], bqc[:], -1.0, None, op0=ALU.mult).then_inc(s_p0b.h, 1)
                e.tensor_scalar(nbk[:], bkc[:], -1.0, None, op0=ALU.mult).then_inc(s_p0b.h, 1)
                s_p0b.n += 2
                e.wait_ge(s_p0b.h, s_p0b.n)
                for j in range(NDT):
                    v = e.tensor_scalar(h128su[:, j * 128:(j + 1) * 128], h128b[:],
                                        sucol[:, j:j + 1], None, op0=ALU.mult)
                s_p0b.inc(v)

        with nc.sbuf_tensor("xnT", [128, NDT * T], BF16) as xnT:
            xnT3 = xnT[:].rearrange("p (d t) -> p d t", d=NDT)

            with ExitStack() as pa:
                psb = lambda *a: pa.enter_context(nc.sbuf_tensor(*a))
                pps = lambda *a: pa.enter_context(nc.psum_tensor(*a))
                xt = [psb(f"xt{i}", [128, D], F32) for i in range(4)]
                stats = [psb(f"stats{i}", [128, 24], F32) for i in range(2)]
                aggr = [psb(f"aggr{i}", [128, 2], F32) for i in range(2)]
                vpe = [psb(f"vpe{i}", [128, 1], F32) for i in range(2)]
                sdv = [psb(f"sdv{i}", [128, 1], F32) for i in range(2)]
                rstd = [psb(f"rstd{i}", [128, 1], F32) for i in range(2)]
                nmean = [psb(f"nmean{i}", [128, 1], F32) for i in range(2)]
                xnnat = [psb(f"xnnat{i}", [128, D], BF16) for i in range(2)]
                ps_tr = [pps(f"ps_atr{i}", [128, 1024], BF16) for i in range(2)]
                blk = pa.enter_context(nc.Block("pa"))

                @blk.sync
                def _(e):
                    if not on("A"): return
                    for tt in range(NDT):
                        if tt >= 4:
                            e.wait_ge(s_xn.h, tt - 3)
                        e.dma_start(xt[tt % 4][:], x_d[tt * 128:(tt + 1) * 128, :]).then_inc(s_ldx[tt % 4].h, 16)
                        s_ldx[tt % 4].n += 16

                @blk.vector
                def _(e):
                    if not on("A"): return

                    def stage1(tt):
                        p = tt % 2
                        e.wait_ge(s_ldx[tt % 4].h, 16 * (tt // 4 + 1))
                        for i in range(4):
                            v = e.bn_stats(stats[p][:, i * 6:(i + 1) * 6], xt[tt % 4][:, i * 512:(i + 1) * 512])
                        s_stats.inc(v)
                        e.wait_ge(s_stats.h, s_stats.n)
                        s_aggr.inc(e.bn_aggr(aggr[p][:], stats[p][:]))
                        e.wait_ge(s_aggr.h, s_aggr.n)
                        s_vpe.inc(e.tensor_scalar(vpe[p][:], aggr[p][:, 1:2], 1.0, 1e-5, op0=ALU.mult, op1=ALU.add))

                    def stage2(tt):
                        p = tt % 2
                        e.wait_ge(s_sd.h, tt + 1)
                        s_rs.inc(e.reciprocal(rstd[p][:], sdv[p][:]))
                        e.wait_ge(s_rs.h, s_rs.n)
                        s_nm.inc(e.tensor_scalar(nmean[p][:], aggr[p][:, 0:1], rstd[p][:], None, op0=ALU.mult))
                        e.wait_ge(s_nm.h, s_nm.n)
                        if tt >= 2:
                            e.wait_ge(s_trF.h, 2 * (tt - 1))
                        s_xn.inc(e.tensor_scalar(xnnat[p][:], xt[tt % 4][:], rstd[p][:], nmean[p][:],
                                                 op0=ALU.mult, op1=ALU.subtract))

                    stage1(0)
                    for tt in range(NDT):
                        if tt + 1 < NDT:
                            stage1(tt + 1)
                        stage2(tt)

                @blk.scalar
                def _(e):
                    if not on("A"): return
                    for tt in range(NDT):
                        e.wait_ge(s_vpe.h, tt + 1)
                        if tt >= 2:
                            e.wait_ge(s_rs.h, tt - 1)
                        s_sd.inc(e.activation(sdv[tt % 2][:], vpe[tt % 2][:], AF.Sqrt, bias=0.0, scale=1.0))
                        for f in range(2):
                            e.wait_ge(s_trF.h, 2 * tt + f + 1)
                            dst = xnT3[:, f * 8:(f + 1) * 8, tt * 128:(tt + 1) * 128]
                            s_trE.inc(e.copy(dst, ps_tr[f][:]))

                @blk.tensor
                def _(e):
                    if not on("A"): return
                    for tt in range(NDT):
                        e.wait_ge(s_xn.h, tt + 1)
                        for f in range(2):
                            if tt >= 1:
                                e.wait_ge(s_trE.h, 2 * tt + f - 1)
                            for d in range(8):
                                dt = f * 8 + d
                                ins = e.transpose(ps_tr[f][:, d * 128:(d + 1) * 128],
                                                  xnnat[tt % 2][:, dt * 128:(dt + 1) * 128], identb[:])
                            s_trF.inc(ins)

            with ExitStack() as pb:
                psb = lambda *a: pb.enter_context(nc.sbuf_tensor(*a))
                pps = lambda *a: pb.enter_context(nc.psum_tensor(*a))
                ones_scan = psb("ones_scan", [128, T], BF16)
                wnat = [[psb(f"wnat{w}{i}", [128, D], BF16) for i in range(2)] for w in range(3)]
                wT = [[psb(f"wT{w}{i}", [128, D], BF16) for i in range(2)] for w in range(3)]
                phiq = psb("phiq", [128, T], F32)
                pv = psb("pv", [128, T], F32)
                kv = psb("kv", [128, T], F32)
                tq_t = psb("tq_t", [128, TC], F32)
                tk_t = psb("tk_t", [128, TC], F32)
                phik_t = psb("phik_t", [128, TC], F32)
                z0a = psb("z0a", [128, NDT * 128], F32)
                z0b = psb("z0b", [128, NDT * 128], F32)
                zst = [psb(f"zst{i}", [128, NDT * 128], BF16) for i in range(2)]
                a_q = [psb(f"a_q{i}", [128, TC], F32) for i in range(2)]
                b_q = [psb(f"b_q{i}", [128, TC], F32) for i in range(2)]
                a_k = [psb(f"a_k{i}", [128, TC], F32) for i in range(2)]
                b_k = [psb(f"b_k{i}", [128, TC], F32) for i in range(2)]
                v_s = [psb(f"v_s{i}", [128, TC], F32) for i in range(2)]
                ctxst = [psb(f"ctxst{i}", [128, T], BF16) for i in range(2)]
                ps_q = [pps(f"ps_q{i}", [128, TC], F32) for i in range(2)]
                ps_k = [pps(f"ps_k{i}", [128, TC], F32) for i in range(2)]
                ps_v = [pps(f"ps_v{i}", [128, TC], F32) for i in range(2)]
                ps_wtr = [pps(f"ps_wtr{i}", [128, 1024], BF16) for i in range(2)]
                ps_z = ps_v
                blk = pb.enter_context(nc.Block("pb"))
                NA = NDT
                wq_r = [wq_d, wk_d, wv_d]

                @blk.gpsimd
                def _(e):
                    if not on("B"): return
                    s_p0a.inc(e.memset(ones_scan[:], 1.0))
                    for a in range(NA):
                        if a >= 2:
                            e.wait_ge(s_wtrF.h, 6 * (a - 1))
                        for w in range(3):
                            e.dma_start(wnat[w][a % 2][:], wq_r[w][a * 128:(a + 1) * 128, :]).then_inc(s_wsl[a % 2].h, 16)
                            s_wsl[a % 2].n += 16

                @blk.tensor
                def _(e):
                    if not on("B"): return

                    def bfills(aa, w):
                        if w == 0:
                            dwait(e, s_wsl, aa, 3)
                        for f in range(2):
                            fill = 6 * aa + w * 2 + f
                            if fill >= 2:
                                e.wait_ge(s_wtrE.h, fill - 1)
                            for d in range(8):
                                dt = f * 8 + d
                                ins = e.transpose(ps_wtr[fill % 2][:, d * 128:(d + 1) * 128],
                                                  wnat[w][aa % 2][:, dt * 128:(dt + 1) * 128], identb[:])
                            s_wtrF.inc(ins)

                    for w in range(3):
                        bfills(0, w)
                    for a in range(NA):
                        e.wait_ge(s_wtrE.h, 6 * (a + 1))
                        for c in range(NTC):
                            k_i = a * NTC + c
                            for w, (ps_w, s_mm) in enumerate([(ps_q, s_mmq), (ps_k, s_mmk), (ps_v, s_mmv)]):
                                if k_i >= 2:
                                    if w == 0:
                                        e.wait_ge(s_aq.h, k_i - 1)
                                        e.wait_ge(s_tq.h, k_i - 1)
                                    elif w == 1:
                                        e.wait_ge(s_ak.h, k_i - 1)
                                        e.wait_ge(s_tk.h, k_i - 1)
                                    else:
                                        e.wait_ge(s_vs.h, k_i - 1)
                                        if c < 2:
                                            e.wait_ge(s_zev.h, 4 * a + c - 1)
                                for d in range(NDT):
                                    ins = e.matmul(ps_w[c % 2][:], wT[w][a % 2][:, d * 128:(d + 1) * 128],
                                                   xnT3[:, d, c * TC:(c + 1) * TC],
                                                   start=(d == 0), stop=(d == NDT - 1))
                                s_mm.inc(ins)
                            if a + 1 < NA and c < 3:
                                bfills(a + 1, c)
                        e.wait_ge(s_vs.h, 4 * (a + 1))
                        for g in range(4):
                            zidx = a * 4 + g
                            if zidx >= 2:
                                e.wait_ge(s_zev.h, zidx - 1)
                            for d in range(4):
                                dt = g * 4 + d
                                ins = e.matmul(ps_z[zidx % 2][:, d * 128:(d + 1) * 128],
                                               h128su[:, dt * 128:(dt + 1) * 128],
                                               xnT3[:, dt, a * 128:(a + 1) * 128],
                                               start=True, stop=True)
                            s_zmm.inc(ins)

                @blk.scalar
                def _(e):
                    if not on("B"): return

                    def bevacs(aa, w):
                        for f in range(2):
                            fill = 6 * aa + w * 2 + f
                            e.wait_ge(s_wtrF.h, fill + 1)
                            s_wtrE.inc(e.copy(wT[w][aa % 2][:, f * 1024:(f + 1) * 1024], ps_wtr[fill % 2][:]))

                    for w in range(3):
                        bevacs(0, w)
                    for a in range(NA):
                        for c in range(NTC):
                            k_i = a * NTC + c
                            p = c % 2
                            e.wait_ge(s_mmq.h, k_i + 1)
                            if k_i >= 2:
                                e.wait_ge(s_tq.h, k_i - 1)
                            s_aq.inc(e.activation(a_q[p][:], ps_q[p][:], AF.Relu, bias=nbq[:, a:a + 1], scale=-1.0))
                            e.wait_ge(s_aq.h, s_aq.n)
                            if k_i >= 2:
                                e.wait_ge(s_phiq.h, k_i - 1)
                            s_bq.inc(e.activation(b_q[p][:], a_q[p][:], AF.Exp, bias=0.0, scale=-1.0))
                            e.wait_ge(s_mmk.h, k_i + 1)
                            if k_i >= 2:
                                e.wait_ge(s_tk.h, k_i - 1)
                            s_ak.inc(e.activation(a_k[p][:], ps_k[p][:], AF.Relu, bias=nbk[:, a:a + 1], scale=-1.0))
                            e.wait_ge(s_ak.h, s_ak.n)
                            if k_i >= 2:
                                e.wait_ge(s_phik.h, k_i - 1)
                            s_bk.inc(e.activation(b_k[p][:], a_k[p][:], AF.Exp, bias=0.0, scale=-1.0))
                            if a + 1 < NA and c < 3:
                                bevacs(a + 1, c)
                        for g in range(4):
                            zidx = a * 4 + g
                            e.wait_ge(s_zmm.h, zidx + 1)
                            if g == 0 and a >= 1:
                                e.wait_ge(s_st3.h, a)
                            s_zev.inc(e.copy(z0a[:, g * 512:(g + 1) * 512], ps_z[zidx % 2][:]))


                @blk.vector
                def _(e):
                    if not on("B"): return
                    e.wait_ge(s_p0a.h, s_p0a.n)
                    for a in range(NA):
                        for c in range(NTC):
                            k_i = a * NTC + c
                            p = c % 2
                            cs = slice(c * TC, (c + 1) * TC)
                            e.wait_ge(s_mmq.h, k_i + 1)
                            e.wait_ge(s_aq.h, k_i + 1)
                            s_tq.inc(e.scalar_tensor_tensor(tq_t[:], ps_q[p][:], bqc[:, a:a + 1],
                                                            a_q[p][:], op0=ALU.add, op1=ALU.add))
                            e.wait_ge(s_tq.h, s_tq.n)
                            e.wait_ge(s_bq.h, k_i + 1)
                            s_phiq.inc(e.tensor_tensor(phiq[:, cs], tq_t[:], b_q[p][:], op=ALU.add))
                            e.wait_ge(s_mmk.h, k_i + 1)
                            e.wait_ge(s_ak.h, k_i + 1)
                            s_tk.inc(e.scalar_tensor_tensor(tk_t[:], ps_k[p][:], bkc[:, a:a + 1],
                                                            a_k[p][:], op0=ALU.add, op1=ALU.add))
                            e.wait_ge(s_tk.h, s_tk.n)
                            e.wait_ge(s_bk.h, k_i + 1)
                            s_phik.inc(e.tensor_tensor(phik_t[:], tk_t[:], b_k[p][:], op=ALU.add))
                            e.wait_ge(s_mmv.h, k_i + 1)
                            s_vs.inc(e.tensor_scalar(v_s[p][:], ps_v[p][:], SCALE, bvs[:, a:a + 1],
                                                     op0=ALU.mult, op1=ALU.add))
                            e.wait_ge(s_phik.h, s_phik.n)
                            e.wait_ge(s_vs.h, s_vs.n)
                            s_pv.inc(e.tensor_tensor(pv[:, cs], phik_t[:], v_s[p][:], op=ALU.mult))
                        e.wait_ge(s_pv.h, s_pv.n)
                        s_kv.inc(e.tensor_tensor_scan(kv[:], ones_scan[:], pv[:], 0.0,
                                                      op0=ALU.mult, op1=ALU.add))
                        e.wait_ge(s_kv.h, s_kv.n)
                        e.wait_ge(s_phiq.h, s_phiq.n)
                        if a >= 2:
                            dwait(e, s_ctxd, a - 2, 1)
                        s_ctx.inc(e.tensor_tensor(ctxst[a % 2][:], phiq[:], kv[:], op=ALU.mult))
                        CHB = 128
                        e.wait_ge(s_zev.h, 4 * (a + 1))
                        if a >= 1:
                            e.wait_ge(s_st4.h, a)
                        if a >= 2:
                            dwait(e, s_zd2, a - 2, 1)
                        for stage, (bsrc, bdst) in enumerate([(z0a, z0b), (z0b, z0a), (z0a, z0b)]):
                            m = [8, 4, 2][stage]
                            for dt in range(NDT):
                                i0, i1 = dt & ~m, dt | m
                                op = ALU.add if (dt & m) == 0 else ALU.subtract
                                v = e.tensor_tensor(bdst[:, dt * CHB:(dt + 1) * CHB],
                                                    bsrc[:, i0 * CHB:(i0 + 1) * CHB],
                                                    bsrc[:, i1 * CHB:(i1 + 1) * CHB], op=op)
                            [s_st1, s_st2, s_st3][stage].inc(v)
                            e.wait_ge([s_st1, s_st2, s_st3][stage].h, [s_st1, s_st2, s_st3][stage].n)
                        for dt in range(NDT):
                            i0, i1 = dt & ~1, dt | 1
                            op = ALU.add if (dt & 1) == 0 else ALU.subtract
                            v = e.tensor_tensor(zst[a % 2][:, dt * CHB:(dt + 1) * CHB],
                                                z0b[:, i0 * CHB:(i0 + 1) * CHB],
                                                z0b[:, i1 * CHB:(i1 + 1) * CHB], op=op)
                        s_st4.inc(v)

                @blk.sync
                def _(e):
                    if not on("B"): return
                    z2v = z2_d.rearrange("p (d t) -> p d t", d=NDT)
                    for a in range(NA):
                        e.wait_ge(s_ctx.h, a + 1)
                        e.dma_start(ctx2_d[a * 128:(a + 1) * 128, :], ctxst[a % 2][:]).then_inc(s_ctxd[a % 2].h, 16)
                        s_ctxd[a % 2].n += 16
                        e.wait_ge(s_st4.h, a + 1)
                        e.dma_start(z2v[:, :, a * 128:(a + 1) * 128], zst[a % 2][:]).then_inc(s_zd2[a % 2].h, 16)
                        s_zd2[a % 2].n += 16
                    e.wait_ge(s_ctxd[0].h, s_ctxd[0].n)
                    e.wait_ge(s_ctxd[1].h, s_ctxd[1].n)
                    e.wait_ge(s_zd2[0].h, s_zd2[0].n)
                    e.wait_ge(s_zd2[1].h, s_zd2[1].n)


            with ExitStack() as pc:
                psb = lambda *a: pc.enter_context(nc.sbuf_tensor(*a))
                pps = lambda *a: pc.enter_context(nc.psum_tensor(*a))
                wonat = [psb(f"wonat{i}", [128, D], BF16) for i in range(2)]
                woT = psb("woT", [128, NDT * D], BF16)
                ctxsl = [psb(f"ctxsl{i}", [128, NDT * TC], BF16) for i in range(2)]
                sst = [psb(f"sst{i}", [128, TC], BF16) for i in range(8)]
                ps_wtr = [pps(f"ps_cwtr{i}", [128, 1024], BF16) for i in range(4)]
                ps_s = [pps(f"ps_s{i}", [128, TC], F32) for i in range(4)]
                blk = pc.enter_context(nc.Block("pc"))
                woT3 = woT[:].rearrange("p (a m q) -> p a m q", a=NDT, m=NDT)
                ctxsl3 = [ctxsl[i].rearrange("p (a t) -> p a t", a=NDT) for i in range(2)]

                @blk.gpsimd
                def _(e):
                    if not on("C"): return
                    for m in range(NDT):
                        if m >= 2:
                            e.wait_ge(s_wotrF.h, 2 * (m - 1))
                        e.dma_start(wonat[m % 2][:], wo_d[m * 128:(m + 1) * 128, :]).then_inc(s_wosl[m % 2].h, 16)
                        s_wosl[m % 2].n += 16
                    for c in range(NTC):
                        for et in range(NDT):
                            e.wait_ge(s_sev.h, c * NDT + et + 1)
                            kk = c * NDT + et
                            e.dma_start(s2d_d[et * 128:(et + 1) * 128, c * TC:(c + 1) * TC],
                                        sst[kk % 8][:]).then_inc(s_sd2[kk % 2].h, 16)
                            s_sd2[kk % 2].n += 16
                    e.wait_ge(s_sd2[0].h, s_sd2[0].n)
                    e.wait_ge(s_sd2[1].h, s_sd2[1].n)

                @blk.sync
                def _(e):
                    if not on("C"): return
                    e.wait_ge(s_ctxd[0].h, s_ctxd[0].n)
                    e.wait_ge(s_ctxd[1].h, s_ctxd[1].n)
                    src = ctx2_d.rearrange("(a p) t -> p a t", p=128)
                    for c in range(NTC):
                        if c >= 2:
                            e.wait_ge(s_mmo.h, 16 * (c - 1))
                        for half in range(4):
                            ins = e.dma_start(ctxsl3[c % 2][:, half * 4:(half + 1) * 4, :],
                                              src[:, half * 4:(half + 1) * 4, c * TC:(c + 1) * TC])
                            ins.then_inc(s_ctxsl[c % 2].h, 16)
                            s_ctxsl[c % 2].n += 16

                @blk.tensor
                def _(e):
                    if not on("C"): return
                    for m in range(NDT):
                        dwait(e, s_wosl, m, 1)
                        for f in range(2):
                            fill = 2 * m + f
                            if fill >= 4:
                                e.wait_ge(s_wotrE.h, fill - 3)
                            for d in range(8):
                                a = f * 8 + d
                                ins = e.transpose(ps_wtr[fill % 4][:, d * 128:(d + 1) * 128],
                                                  wonat[m % 2][:, a * 128:(a + 1) * 128], identb[:])
                            s_wotrF.inc(ins)
                    for c in range(NTC):
                        dwait(e, s_ctxsl, c, 4)
                        for et in range(NDT):
                            k_i = c * NDT + et
                            bank = et % 4
                            e.wait_ge(s_wotrE.h, 2 * (et + 1))
                            if k_i >= 4:
                                e.wait_ge(s_sev.h, k_i - 3)
                            for a in range(NDT):
                                ins = e.matmul(ps_s[bank][:], woT3[:, a, et, :],
                                               ctxsl3[c % 2][:, a, :], start=(a == 0), stop=(a == NDT - 1))
                            s_mmo.inc(ins)

                @blk.scalar
                def _(e):
                    if not on("C"): return
                    for m in range(NDT):
                        for f in range(2):
                            fill = 2 * m + f
                            e.wait_ge(s_wotrF.h, fill + 1)
                            dst = woT3[:, f * 8:(f + 1) * 8, m, :]
                            s_wotrE.inc(e.copy(dst, ps_wtr[fill % 4][:]))

                @blk.vector
                def _(e):
                    if not on("C"): return
                    for c in range(NTC):
                        for et in range(NDT):
                            k_i = c * NDT + et
                            e.wait_ge(s_mmo.h, k_i + 1)
                            if k_i >= 8:
                                dwait(e, s_sd2, k_i - 2, 1)
                            s_sev.inc(e.tensor_scalar(sst[k_i % 8][:], ps_s[et % 4][:], 1.0,
                                                      boc[:, et:et + 1], op0=ALU.mult, op1=ALU.add))

            with nc.sbuf_tensor("z", [128, NDT * T], BF16) as z:
                z3 = z[:].rearrange("p (d t) -> p d t", d=NDT)

                if debug:
                    with ExitStack() as pdbg:
                        blk = pdbg.enter_context(nc.Block("pdbg"))
                        sdbg = Ctr(pdbg, nc, "sdbg")

                        @blk.sync
                        def _(e):
                            if not on("Z"): return
                            e.dma_start(dbg_xnT_d, xnT[:]).then_inc(sdbg.h, 16)
                            e.dma_start(dbg_z_d, z[:]).then_inc(sdbg.h, 16)
                            e.wait_ge(sdbg.h, 32)

                with ExitStack() as pe:
                    psb = lambda *a: pe.enter_context(nc.sbuf_tensor(*a))
                    pps = lambda *a: pe.enter_context(nc.psum_tensor(*a))
                    wupn = [psb(f"wupn{i}", [128, D], BF16) for i in range(2)]
                    wgn = [psb(f"wgn{i}", [128, D], BF16) for i in range(2)]
                    wupT = [psb(f"wupT{i}", [128, D], BF16) for i in range(2)]
                    wgT = [psb(f"wgT{i}", [128, D], BF16) for i in range(2)]
                    u_t = [psb(f"u_t{i}", [128, T], BF16) for i in range(2)]
                    ystage = [psb(f"ystage{i}", [128, T], BF16) for i in range(2)]
                    sg_t = [psb(f"sg_t{i}", [128, TC], F32) for i in range(2)]
                    ps_up = [pps(f"ps_up{i}", [128, TC], F32) for i in range(2)]
                    ps_g = [pps(f"ps_g{i}", [128, TC], F32) for i in range(2)]
                    ps_y = [pps(f"ps_y{i}", [128, TC], F32) for i in range(2)]
                    ps_wtr = [pps(f"ps_ewtr{i}", [128, 1024], BF16) for i in range(2)]
                    blk = pe.enter_context(nc.Block("pe"))
                    wup_r = wup_d.rearrange("(c j) d -> j c d", j=64)
                    wg_r = wg_d.rearrange("(c j) d -> j c d", j=64)
                    y2_r = y2_d.rearrange("(c j) t -> j c t", j=64)

                    @blk.gpsimd
                    def _(e):
                        if not on("E"): return
                        for j in range(64):
                            if j >= 2:
                                e.wait_ge(s_ewtrF.h, 4 * (j - 1))
                            e.dma_start(wupn[j % 2][:], wup_r[j]).then_inc(s_ewsl[j % 2].h, 16)
                            e.dma_start(wgn[j % 2][:], wg_r[j]).then_inc(s_ewsl[j % 2].h, 16)
                            s_ewsl[j % 2].n += 32

                    @blk.tensor
                    def _(e):
                        if not on("E"): return
                        pend = []

                        def emit_h1(jj, cc):
                            kk = 4 * jj + cc
                            e.wait_ge(s_eu.h, kk + 1)
                            if kk >= 2:
                                e.wait_ge(s_eyev.h, kk - 1)
                            ins = e.matmul(ps_y[cc % 2][:], h128b[:],
                                           u_t[jj % 2][:, cc * TC:(cc + 1) * TC], start=True, stop=True)
                            s_emy.inc(ins)

                        def fills(jj, pair):
                            if pair == 0:
                                dwait(e, s_ewsl, jj, 2)
                            wn = wupn if pair == 0 else wgn
                            for f in range(2):
                                fill = 4 * jj + pair * 2 + f
                                if fill >= 2:
                                    e.wait_ge(s_ewtrE.h, fill - 1)
                                for d in range(8):
                                    dt = pair * 0 + f * 8 + d
                                    ins = e.transpose(ps_wtr[fill % 2][:, d * 128:(d + 1) * 128],
                                                      wn[jj % 2][:, dt * 128:(dt + 1) * 128], identb[:])
                                s_ewtrF.inc(ins)

                        def chunk_mms(j, c):
                            k_i = 4 * j + c
                            if k_i >= 2:
                                e.wait_ge(s_eu.h, k_i - 1)
                            for d in range(NDT):
                                ins = e.matmul(ps_up[c % 2][:], wupT[j % 2][:, d * 128:(d + 1) * 128],
                                               z3[:, d, c * TC:(c + 1) * TC], start=(d == 0), stop=(d == NDT - 1))
                            s_emu.inc(ins)
                            if k_i >= 2:
                                e.wait_ge(s_esg.h, k_i - 1)
                            for d in range(NDT):
                                ins = e.matmul(ps_g[c % 2][:], wgT[j % 2][:, d * 128:(d + 1) * 128],
                                               z3[:, d, c * TC:(c + 1) * TC], start=(d == 0), stop=(d == NDT - 1))
                            s_emg.inc(ins)
                            pend.append((j, c))
                            if len(pend) >= 2:
                                emit_h1(*pend.pop(0))

                        fills(0, 0)
                        fills(0, 1)
                        e.wait_ge(s_zld.h, 16)
                        for j in range(64):
                            e.wait_ge(s_ewtrE.h, 4 * (j + 1))
                            chunk_mms(j, 0)
                            chunk_mms(j, 1)
                            if j + 1 < 64:
                                fills(j + 1, 0)
                            chunk_mms(j, 2)
                            if j + 1 < 64:
                                fills(j + 1, 1)
                            chunk_mms(j, 3)
                        for jj, cc in pend:
                            emit_h1(jj, cc)

                    @blk.scalar
                    def _(e):
                        if not on("E"): return
                        pendy = []

                        def emit_yev(jj, cc):
                            kk = 4 * jj + cc
                            e.wait_ge(s_emy.h, kk + 1)
                            if jj >= 2 and cc == 0:
                                dwait(e, s_eyd, jj - 2, 1)
                            s_eyev.inc(e.copy(ystage[jj % 2][:, cc * TC:(cc + 1) * TC], ps_y[cc % 2][:]))

                        def wtevacs(jj, pair):
                            for f in range(2):
                                fill = 4 * jj + pair * 2 + f
                                e.wait_ge(s_ewtrF.h, fill + 1)
                                dstT = (wupT if pair == 0 else wgT)[jj % 2]
                                s_ewtrE.inc(e.copy(dstT[:, f * 1024:(f + 1) * 1024], ps_wtr[fill % 2][:]))

                        wtevacs(0, 0)
                        wtevacs(0, 1)
                        for j in range(64):
                            for c in range(NTC):
                                k_i = 4 * j + c
                                e.wait_ge(s_emg.h, k_i + 1)
                                if k_i >= 2:
                                    e.wait_ge(s_eu.h, k_i - 1)
                                s_esg.inc(e.activation(sg_t[c % 2][:], ps_g[c % 2][:], AF.Silu,
                                                       bias=0.0, scale=gsc[:, j:j + 1]))
                                pendy.append((j, c))
                                if len(pendy) > 2:
                                    emit_yev(*pendy.pop(0))
                                if c == 1 and j + 1 < 64:
                                    wtevacs(j + 1, 0)
                                if c == 2 and j + 1 < 64:
                                    wtevacs(j + 1, 1)
                        for jj, cc in pendy:
                            emit_yev(jj, cc)

                    @blk.vector
                    def _(e):
                        if not on("E"): return
                        for j in range(64):
                            for c in range(NTC):
                                k_i = 4 * j + c
                                e.wait_ge(s_emu.h, k_i + 1)
                                e.wait_ge(s_esg.h, k_i + 1)
                                if c == 0 and j >= 2:
                                    e.wait_ge(s_emy.h, 4 * (j - 1))
                                s_eu.inc(e.scalar_tensor_tensor(u_t[j % 2][:, c * TC:(c + 1) * TC],
                                                                ps_up[c % 2][:], usc[:, j:j + 1],
                                                                sg_t[c % 2][:], op0=ALU.mult, op1=ALU.mult))

                    @blk.sync
                    def _(e):
                        if not on("E"): return
                        e.wait_ge(s_zd2[0].h, s_zd2[0].n)
                        e.wait_ge(s_zd2[1].h, s_zd2[1].n)
                        e.dma_start(z[:], z2_d).then_inc(s_zld.h, 16)
                        s_zld.n += 16
                        for j in range(64):
                            e.wait_ge(s_eyev.h, 4 * (j + 1))
                            e.dma_start(y2_r[j], ystage[j % 2][:]).then_inc(s_eyd[j % 2].h, 16)
                            s_eyd[j % 2].n += 16
                        e.wait_ge(s_eyd[0].h, s_eyd[0].n)
                        e.wait_ge(s_eyd[1].h, s_eyd[1].n)

        with ExitStack() as pf:
            wdTp = pf.enter_context(nc.sbuf_tensor("wdTp", [128, 8 * 64 * 128], BF16))
            wdTp3 = wdTp[:].rearrange("p (et a q) -> p et a q", et=8, a=64)
            for h in range(2):
                with ExitStack() as pfp:
                    psb = lambda *a: pfp.enter_context(nc.sbuf_tensor(*a))
                    pps = lambda *a: pfp.enter_context(nc.psum_tensor(*a))
                    wdn = [psb(f"wdn{h}{i}", [128, II], BF16) for i in range(2)]
                    wdraw = [psb(f"wdraw{h}{i}", [128, 1024], BF16) for i in range(8)]
                    ps_ftr = [pps(f"ps_ftr{h}{i}", [128, 1024], BF16) for i in range(4)]
                    ps_fs2 = [pps(f"ps_fs2{h}{i}", [128, 512], F32) for i in range(4)]
                    blk = pfp.enter_context(nc.Block(f"pfp{h}"))
                    base_tr = s_ftrF.n
                    s_fs2m_base = s_fs2m.n
                    s_fs2e_base = s_fs2e.n
                    base_et = 8 * h

                    @blk.gpsimd
                    def _(e):
                        if not on("F"): return
                        for et in range(8):
                            if et >= 2:
                                e.wait_ge(s_ftrF.h, base_tr + 8 * (et - 1))
                            ge = 8 * h + et
                            e.dma_start(wdn[et % 2][:],
                                        wd_d[(base_et + et) * 128:(base_et + et + 1) * 128, :]).then_inc(s_fwsl[ge % 2].h, 16)
                            s_fwsl[ge % 2].n += 16

                    @blk.tensor
                    def _(e):
                        if not on("F"): return
                        pends = []

                        def emit_s2(fill):
                            e.wait_ge(s_ftrE.h, fill + 1)
                            lf = (fill - base_tr) % 8
                            for g2 in range(2):
                                idx2 = 2 * (fill - base_tr) + g2
                                if idx2 >= 4:
                                    e.wait_ge(s_fs2e.h, s_fs2e_base + idx2 - 3)
                                for d in range(4):
                                    dd = g2 * 4 + d
                                    ins = e.matmul(ps_fs2[idx2 % 4][:, d * 128:(d + 1) * 128], s2b[:],
                                                   wdraw[lf][:, dd * 128:(dd + 1) * 128],
                                                   start=True, stop=True)
                                s_fs2m.inc(ins)

                        for et in range(8):
                            dwait(e, s_fwsl, 8 * h + et, 1)
                            for f in range(8):
                                fill = base_tr + 8 * et + f
                                if fill >= base_tr + 4:
                                    e.wait_ge(s_ftrE.h, fill - 3)
                                for d in range(8):
                                    a2 = f * 8 + d
                                    ins = e.transpose(ps_ftr[fill % 4][:, d * 128:(d + 1) * 128],
                                                      wdn[et % 2][:, a2 * 128:(a2 + 1) * 128], identb[:])
                                s_ftrF.inc(ins)
                                pends.append(fill)
                                if len(pends) > 4:
                                    emit_s2(pends.pop(0))
                        for fill in pends:
                            emit_s2(fill)

                    @blk.scalar
                    def _(e):
                        if not on("F"): return
                        pend2 = []

                        def emit_s2e(locfill):
                            et2, f2 = divmod(locfill, 8)
                            for g2 in range(2):
                                idx2 = 2 * locfill + g2
                                e.wait_ge(s_fs2m.h, s_fs2m_base + idx2 + 1)
                                dst = wdTp3[:, et2, f2 * 8 + g2 * 4:f2 * 8 + (g2 + 1) * 4, :]
                                s_fs2e.inc(e.copy(dst, ps_fs2[idx2 % 4][:]))

                        for et in range(8):
                            for f in range(8):
                                fill = base_tr + 8 * et + f
                                e.wait_ge(s_ftrF.h, fill + 1)
                                if et >= 1:
                                    e.wait_ge(s_fs2m.h, s_fs2m_base + 2 * (8 * (et - 1) + f) + 2)
                                s_ftrE.inc(e.copy(wdraw[f][:], ps_ftr[fill % 4][:]))
                                pend2.append(fill - base_tr)
                                if len(pend2) > 4:
                                    emit_s2e(pend2.pop(0))
                        for locfill in pend2:
                            emit_s2e(locfill)

                with ExitStack() as pfs:
                    psb = lambda *a: pfs.enter_context(nc.sbuf_tensor(*a))
                    pps = lambda *a: pfs.enter_context(nc.psum_tensor(*a))
                    yslab = [psb(f"yslab{h}{i}", [128, 64 * FTC], BF16) for i in range(2)]
                    yslab3 = [yslab[i].rearrange("p (a t) -> p a t", a=64) for i in range(2)]
                    gst = [psb(f"gst{h}{i}", [128, FTC], BF16) for i in range(4)]
                    ps_dn = [pps(f"ps_dn{h}{i}", [128, FTC], F32) for i in range(4)]
                    blk = pfs.enter_context(nc.Block(f"pfs{h}"))
                    mmbase = s_fmm.n
                    gbase = s_fgev.n
                    y2r = y2_d.rearrange("(a p) t -> p a t", p=128)

                    @blk.sync
                    def _(e):
                        if not on("F"): return
                        e.wait_ge(s_eyd[0].h, s_eyd[0].n)
                        e.wait_ge(s_eyd[1].h, s_eyd[1].n)
                        for ts in range(NFT):
                            if ts >= 2:
                                e.wait_ge(s_fmm.h, mmbase + 8 * (ts - 1))
                            gt = 8 * h + ts
                            for part in range(8):
                                ins = e.dma_start(yslab3[ts % 2][:, part * 8:(part + 1) * 8, :],
                                                  y2r[:, part * 8:(part + 1) * 8, ts * FTC:(ts + 1) * FTC])
                                ins.then_inc(s_fysl[gt % 2].h, 16)
                                s_fysl[gt % 2].n += 16

                    @blk.tensor
                    def _(e):
                        if not on("F"): return
                        for ts in range(NFT):
                            dwait(e, s_fysl, 8 * h + ts, 8)
                            for et in range(8):
                                idx = 8 * ts + et
                                if idx >= 4:
                                    e.wait_ge(s_fgev.h, gbase + idx - 3)
                                for a2 in range(64):
                                    ins = e.matmul(ps_dn[et % 4][:], wdTp3[:, et, a2, :],
                                                   yslab3[ts % 2][:, a2, :],
                                                   start=(a2 == 0), stop=(a2 == 63))
                                s_fmm.inc(ins)

                    @blk.scalar
                    def _(e):
                        if not on("F"): return
                        for ts in range(NFT):
                            for et in range(8):
                                idx = 8 * ts + et
                                e.wait_ge(s_fmm.h, mmbase + idx + 1)
                                if idx >= 4:
                                    dwait(e, s_fgd, 64 * h + idx - 2, 1)
                                s_fgev.inc(e.activation(gst[idx % 4][:], ps_dn[et % 4][:], AF.Silu,
                                                        bias=0.0, scale=dsc[:, 8 * h + et:8 * h + et + 1]))

                    @blk.gpsimd
                    def _(e):
                        if not on("F"): return
                        for ts in range(NFT):
                            for et in range(8):
                                idx = 8 * ts + et
                                e.wait_ge(s_fgev.h, gbase + idx + 1)
                                gg = 64 * h + idx
                                erow = (8 * h + et) * 128
                                e.dma_start(gd_d[erow:erow + 128, ts * FTC:(ts + 1) * FTC],
                                            gst[idx % 4][:]).then_inc(s_fgd[gg % 2].h, 16)
                                s_fgd[gg % 2].n += 16
                        e.wait_ge(s_fgd[0].h, s_fgd[0].n)
                        e.wait_ge(s_fgd[1].h, s_fgd[1].n)

        with ExitStack() as pg:
            psb = lambda *a: pg.enter_context(nc.sbuf_tensor(*a))
            pps = lambda *a: pg.enter_context(nc.psum_tensor(*a))
            ssl = [psb(f"ssl{i}", [128, NDT * TC], BF16) for i in range(2)]
            gsl = [psb(f"gsl{i}", [128, NDT * TC], BF16) for i in range(2)]
            rT = [psb(f"rT{i}", [128, NDT * TC], BF16) for i in range(2)]
            xt2 = [psb(f"xt2{i}", [128, D], F32) for i in range(2)]
            rnat = [psb(f"rnat{i}", [128, D], BF16) for i in range(2)]
            onat = [psb(f"onat{i}", [128, D], F32) for i in range(2)]
            ps_gtr = [pps(f"ps_gtr{i}", [128, 1024], BF16) for i in range(2)]
            blk = pg.enter_context(nc.Block("pg"))
            rT3 = [rT[i].rearrange("p (a t) -> p a t", a=NDT) for i in range(2)]
            s2r = s2d_d.rearrange("(a p) t -> p a t", p=128)
            gdr = gd_d.rearrange("(a p) t -> p a t", p=128)

            @blk.sync
            def _(e):
                if not on("G"): return
                e.wait_ge(s_sd2[0].h, s_sd2[0].n)
                e.wait_ge(s_sd2[1].h, s_sd2[1].n)
                for tg in range(NTC):
                    if tg >= 2:
                        e.wait_ge(s_grm.h, tg - 1)
                    for part in range(4):
                        e.dma_start(ssl[tg % 2].rearrange("p (a t) -> p a t", a=NDT)[:, part * 4:(part + 1) * 4, :],
                                    s2r[:, part * 4:(part + 1) * 4, tg * TC:(tg + 1) * TC]).then_inc(s_gsl[tg % 2].h, 16)
                        s_gsl[tg % 2].n += 16


            @blk.gpsimd
            def _(e):
                if not on("G"): return
                e.wait_ge(s_fgd[0].h, s_fgd[0].n)
                e.wait_ge(s_fgd[1].h, s_fgd[1].n)
                for tg in range(NTC):
                    if tg >= 2:
                        e.wait_ge(s_grm.h, tg - 1)
                    for part in range(4):
                        e.dma_start(gsl[tg % 2].rearrange("p (a t) -> p a t", a=NDT)[:, part * 4:(part + 1) * 4, :],
                                    gdr[:, part * 4:(part + 1) * 4, tg * TC:(tg + 1) * TC]).then_inc(s_gsl2[tg % 2].h, 16)
                        s_gsl2[tg % 2].n += 16

            @blk.vector
            def _(e):
                if not on("G"): return
                for tg in range(NTC + 1):
                    if tg < NTC:
                        dwait(e, s_gsl, tg, 4)
                        dwait(e, s_gsl2, tg, 4)
                        if tg >= 2:
                            e.wait_ge(s_gtrF.h, 8 * (tg - 1))
                        s_grm.inc(e.tensor_tensor(rT[tg % 2][:], ssl[tg % 2][:], gsl[tg % 2][:], op=ALU.mult))
                    if tg >= 1:
                        for q in range(4):
                            ti = (tg - 1) * 4 + q
                            dwait(e, s_gx, ti, 1)
                            e.wait_ge(s_gtrE.h, 2 * (ti + 1))
                            if ti >= 2:
                                dwait(e, s_gout, ti - 2, 1)
                            s_gadd.inc(e.tensor_tensor(onat[ti % 2][:], xt2[ti % 2][:], rnat[ti % 2][:], op=ALU.add))

            @blk.tensor
            def _(e):
                if not on("G"): return
                for tg in range(NTC):
                    e.wait_ge(s_grm.h, tg + 1)
                    for q in range(4):
                        ti = tg * 4 + q
                        for f in range(2):
                            fill = 2 * ti + f
                            if fill >= 2:
                                e.wait_ge(s_gtrE.h, fill - 1)
                            for d in range(8):
                                et = f * 8 + d
                                ins = e.transpose(ps_gtr[fill % 2][:, d * 128:(d + 1) * 128],
                                                  rT3[tg % 2][:, et, q * 128:(q + 1) * 128], identb[:])
                            s_gtrF.inc(ins)

            @blk.scalar
            def _(e):
                if not on("G"): return
                for ti in range(2):
                    e.dma_start(xt2[ti % 2][:], x_d[ti * 128:(ti + 1) * 128, :]).then_inc(s_gx[ti % 2].h, 16)
                    s_gx[ti % 2].n += 16
                for ti in range(NDT):
                    for f in range(2):
                        fill = 2 * ti + f
                        e.wait_ge(s_gtrF.h, fill + 1)
                        if ti >= 2 and f == 0:
                            e.wait_ge(s_gadd.h, ti - 1)
                        s_gtrE.inc(e.copy(rnat[ti % 2][:, f * 1024:(f + 1) * 1024], ps_gtr[fill % 2][:]))
                    if ti + 2 < NDT:
                        tj = ti + 2
                        e.wait_ge(s_gadd.h, tj - 1)
                        e.dma_start(xt2[tj % 2][:], x_d[tj * 128:(tj + 1) * 128, :]).then_inc(s_gx[tj % 2].h, 16)
                        s_gx[tj % 2].n += 16

            @blk.gpsimd
            def _(e):
                if not on("G"): return
                for ti in range(NDT):
                    e.wait_ge(s_gadd.h, ti + 1)
                    e.dma_start(out_d[ti * 128:(ti + 1) * 128, :], onat[ti % 2][:]).then_inc(s_gout[ti % 2].h, 16)
                    s_gout[ti % 2].n += 16
                e.wait_ge(s_gout[0].h, s_gout[0].n)
                e.wait_ge(s_gout[1].h, s_gout[1].n)

    return nc


_H128 = _had(128).astype(BF)
_S2C = np.kron(np.eye(2, dtype=np.float32), _had(64)).astype(BF)
_IDENT = np.eye(128, dtype=np.float32)


def kernel(L0_attn_x, L0_attn_Wq, L0_attn_bq, L0_attn_Wk, L0_attn_bk, L0_attn_Wv,
           L0_attn_bv, L0_attn_Wo, L0_attn_bo, L0_ffn_SU_ug, L0_ffn_SU_dp,
           L0_ffn_Wscale_ug, L0_ffn_Wscale_dp, L0_ffn_W_up, L0_ffn_W_gate, L0_ffn_W_down):
    nc = build_program(debug=False)
    x = np.ascontiguousarray(np.asarray(L0_attn_x, dtype=np.float32))
    shared = {
        "wq": np.ascontiguousarray(L0_attn_Wq, dtype=np.float32),
        "wk": np.ascontiguousarray(L0_attn_Wk, dtype=np.float32),
        "wv": np.ascontiguousarray(L0_attn_Wv, dtype=np.float32),
        "wo": np.ascontiguousarray(L0_attn_Wo, dtype=np.float32),
        "bq": np.ascontiguousarray(L0_attn_bq, dtype=np.float32),
        "bk": np.ascontiguousarray(L0_attn_bk, dtype=np.float32),
        "bv": np.ascontiguousarray(L0_attn_bv, dtype=np.float32),
        "bo": np.ascontiguousarray(L0_attn_bo, dtype=np.float32),
        "su_ug": np.ascontiguousarray(L0_ffn_SU_ug, dtype=np.float32),
        "su_dp": np.ascontiguousarray(L0_ffn_SU_dp, dtype=np.float32),
        "wsc_ug": np.ascontiguousarray(L0_ffn_Wscale_ug, dtype=np.float32),
        "wsc_dp": np.ascontiguousarray(L0_ffn_Wscale_dp, dtype=np.float32),
        "wup": np.ascontiguousarray(L0_ffn_W_up, dtype=np.float32),
        "wg": np.ascontiguousarray(L0_ffn_W_gate, dtype=np.float32),
        "wd": np.ascontiguousarray(L0_ffn_W_down, dtype=np.float32),
        "h128": _H128, "s2c": _S2C,
        "identc": _IDENT.astype(BF), "identf": _IDENT,
    }
    in_maps = [dict(shared, x=np.ascontiguousarray(x[c])) for c in range(8)]
    res = run_bass_kernel_spmd(nc, in_maps, core_ids=list(range(8)))
    return np.stack([res.results[c]["out"] for c in range(8)], axis=0)


# version 22
# speedup vs baseline: 1.1910x; 1.1910x over previous
import numpy as np
import ml_dtypes
from contextlib import ExitStack

import concourse.bass as bass
import concourse.mybir as mybir
from concourse.bass_utils import run_bass_kernel_spmd

F32 = mybir.dt.float32
BF16 = mybir.dt.bfloat16
AF = mybir.ActivationFunctionType
ALU = mybir.AluOpType

T = 2048
D = 2048
II = 8192
TC = 512
NTC = T // TC
NDT = D // 128
NIT = II // 64
CH = 256
NCH = T // CH
FTC = 256
NFT = T // FTC
SCALE = float(128 ** -0.5)
R2048 = float(1.0 / np.sqrt(2048.0))
R8192 = float(1.0 / np.sqrt(8192.0))
BF = ml_dtypes.bfloat16


def _had(n):
    h = np.array([[1.0]], dtype=np.float32)
    while h.shape[0] < n:
        h = np.block([[h, h], [h, -h]])
    return h


class Ctr:
    def __init__(self, ctx, nc, name):
        self.h = ctx.enter_context(nc.semaphore(name))
        self.n = 0

    def inc(self, instr, k=1):
        instr.then_inc(self.h, k)
        self.n += k
        return self.n


def dwait(e, sems, r, g):
    if r >= 0:
        e.wait_ge(sems[r % 2].h, 16 * g * (r // 2 + 1))


def build_program(debug=False, upto="G"):
    _ORD = ["0", "A", "B", "C", "Z", "E", "F", "G"]
    _lim = _ORD.index(upto)
    on = lambda ph: _ORD.index(ph) <= _lim
    nc = bass.Bass("TRN2", target_bir_lowering=False)

    x_d = nc.dram_tensor("x", [T, D], F32, kind="ExternalInput").ap()
    wq_d = nc.dram_tensor("wq", [D, D], F32, kind="ExternalInput").ap()
    wk_d = nc.dram_tensor("wk", [D, D], F32, kind="ExternalInput").ap()
    wv_d = nc.dram_tensor("wv", [D, D], F32, kind="ExternalInput").ap()
    wo_d = nc.dram_tensor("wo", [D, D], F32, kind="ExternalInput").ap()
    bq_d = nc.dram_tensor("bq", [D], F32, kind="ExternalInput").ap()
    bk_d = nc.dram_tensor("bk", [D], F32, kind="ExternalInput").ap()
    bv_d = nc.dram_tensor("bv", [D], F32, kind="ExternalInput").ap()
    bo_d = nc.dram_tensor("bo", [D], F32, kind="ExternalInput").ap()
    suu_d = nc.dram_tensor("su_ug", [D], F32, kind="ExternalInput").ap()
    sud_d = nc.dram_tensor("su_dp", [II], F32, kind="ExternalInput").ap()
    wscu_d = nc.dram_tensor("wsc_ug", [2 * II], F32, kind="ExternalInput").ap()
    wscd_d = nc.dram_tensor("wsc_dp", [D], F32, kind="ExternalInput").ap()
    wup_d = nc.dram_tensor("wup", [II, D], F32, kind="ExternalInput").ap()
    wg_d = nc.dram_tensor("wg", [II, D], F32, kind="ExternalInput").ap()
    wd_d = nc.dram_tensor("wd", [D, II], F32, kind="ExternalInput").ap()
    h128_d = nc.dram_tensor("h128", [128, 128], BF16, kind="ExternalInput").ap()
    s2_d = nc.dram_tensor("s2c", [128, 128], BF16, kind="ExternalInput").ap()
    ident_d = nc.dram_tensor("identc", [128, 128], BF16, kind="ExternalInput").ap()
    identf_d = nc.dram_tensor("identf", [128, 128], F32, kind="ExternalInput").ap()

    out_d = nc.dram_tensor("out", [T, D], F32, kind="ExternalOutput").ap()

    scratch_kind = "ExternalOutput" if debug else "Internal"
    ctx2_d = nc.dram_tensor("ctx2", [D, T], BF16, kind=scratch_kind).ap()
    s2d_d = nc.dram_tensor("s2d", [D, T], BF16, kind=scratch_kind).ap()
    y2_d = nc.dram_tensor("y2", [II, T], BF16, kind=scratch_kind).ap()
    gd_d = nc.dram_tensor("gd", [D, T], BF16, kind=scratch_kind).ap()
    z2_d = nc.dram_tensor("z2", [128, NDT * T], BF16, kind=scratch_kind).ap()
    if debug:
        dbg_xnT_d = nc.dram_tensor("dbg_xnT", [128, NDT * T], BF16, kind="ExternalOutput").ap()
        dbg_z_d = nc.dram_tensor("dbg_z", [128, NDT * T], BF16, kind="ExternalOutput").ap()

    gctx = ExitStack()
    with gctx:
        sb = lambda *a: gctx.enter_context(nc.sbuf_tensor(*a))
        sem = lambda name: Ctr(gctx, nc, name)

        h128b = sb("h128b", [128, 128], BF16)
        s2b = sb("s2b", [128, 128], BF16)
        identb = sb("identb", [128, 128], BF16)
        identf = sb("identfb", [128, 128], F32)
        h128su = sb("h128su", [128, NDT * 128], BF16)
        usc = sb("usc", [128, 64], F32)
        gsc = sb("gsc", [128, 64], F32)
        dsc = sb("dsc", [128, 16], F32)
        sucol = sb("sucol", [128, 16], F32)
        nbq = sb("nbq", [128, 16], F32)
        nbk = sb("nbk", [128, 16], F32)
        bqc = sb("bqc", [128, 16], F32)
        bkc = sb("bkc", [128, 16], F32)
        bvs = sb("bvs", [128, 16], F32)
        boc = sb("boc", [128, 16], F32)

        s_ld0 = sem("ld0"); s_ldg = sem("ldg"); s_p0a = sem("p0a"); s_p0b = sem("p0b")
        s_ldx = [sem("ldx0"), sem("ldx1"), sem("ldx2"), sem("ldx3")]; s_stats = sem("stats"); s_aggr = sem("aggr")
        s_vpe = sem("vpe"); s_sd = sem("sd"); s_rs = sem("rs"); s_nm = sem("nm")
        s_xn = sem("xn"); s_trF = sem("trF"); s_trE = sem("trE")
        s_wsl = [sem("wsl0"), sem("wsl1")]; s_wtrF = sem("wtrF"); s_wtrE = sem("wtrE")
        s_mmq = sem("mmq"); s_mmk = sem("mmk"); s_mmv = sem("mmv")
        s_aq = sem("aq"); s_bq = sem("bq"); s_ak = sem("ak"); s_bk = sem("bk")
        s_vs = sem("vs"); s_tq = sem("tq"); s_phiq = sem("phiq"); s_tk = sem("tk")
        s_phik = sem("phik"); s_pv = sem("pv"); s_kv = sem("kv")
        s_ctx = sem("ctx"); s_ctxd = [sem("ctxd0"), sem("ctxd1")]
        s_wosl = [sem("wosl0"), sem("wosl1")]; s_wotrF = sem("wotrF"); s_wotrE = sem("wotrE")
        s_ctxsl = [sem("ctxsl0"), sem("ctxsl1")]; s_mmo = sem("mmo"); s_sev = sem("sev")
        s_sd2 = [sem("sd20"), sem("sd21")]
        s_zmm = sem("zmm"); s_zev = sem("zev"); s_zd2 = [sem("zd20"), sem("zd21")]; s_zld = sem("zld")
        s_st1 = sem("st1"); s_st2 = sem("st2"); s_st3 = sem("st3"); s_st4 = sem("st4")
        s_ewsl = [sem("ewsl0"), sem("ewsl1")]; s_ewtrF = sem("ewtrF"); s_ewtrE = sem("ewtrE")
        s_emu = sem("emu"); s_emg = sem("emg"); s_emy = sem("emy")
        s_esg = sem("esg"); s_eu = sem("eu"); s_eyev = sem("eyev")
        s_eyd = [sem("eyd0"), sem("eyd1")]
        s_fwsl = [sem("fwsl0"), sem("fwsl1")]; s_ftrF = sem("ftrF"); s_ftrE = sem("ftrE")
        s_fs2m = sem("fs2m"); s_fs2e = sem("fs2e")
        s_fysl = [sem("fysl0"), sem("fysl1")]; s_fmm = sem("fmm"); s_fgev = sem("fgev")
        s_fgd = [sem("fgd0"), sem("fgd1")]
        s_gsl = [sem("gsl0"), sem("gsl1")]; s_gsl2 = [sem("gsl20"), sem("gsl21")]; s_grm = sem("grm"); s_gtrF = sem("gtrF"); s_gtrE = sem("gtrE")
        s_gx = [sem("gx0"), sem("gx1"), sem("gx2"), sem("gx3")]; s_gadd = sem("gadd")
        s_gout = [sem("gout0"), sem("gout1"), sem("gout2"), sem("gout3")]

        with ExitStack() as p0:
            psb = lambda *a: p0.enter_context(nc.sbuf_tensor(*a))
            pps = lambda *a: p0.enter_context(nc.psum_tensor(*a))
            su16 = psb("su16", [16, 128], BF16)
            b16 = psb("b16", [16, 4 * 128], F32)
            sud16 = psb("sud16", [128, 64], F32)
            wsdp16 = psb("wsdp16", [16, 128], F32)
            ps_tr0 = pps("ps_tr0", [128, 16], F32)
            ps_trb = pps("ps_trb", [128, 64], F32)
            ps_tr1 = pps("ps_tr1", [128, 16], BF16)
            sucol_b = psb("sucol_b", [128, 16], BF16)
            blk = p0.enter_context(nc.Block("p0"))

            @blk.sync
            def _(e):
                if not on("0"): return
                e.dma_start(h128b[:], h128_d).then_inc(s_ld0.h, 16)
                e.dma_start(s2b[:], s2_d).then_inc(s_ld0.h, 16)
                e.dma_start(identb[:], ident_d).then_inc(s_ld0.h, 16)
                e.dma_start(identf[:], identf_d).then_inc(s_ld0.h, 16)
                e.dma_start(usc[:], wscu_d.rearrange("(h c j) -> h c j", h=2, j=64)[0]).then_inc(s_ld0.h, 16)
                e.dma_start(gsc[:], wscu_d.rearrange("(h c j) -> h c j", h=2, j=64)[1]).then_inc(s_ld0.h, 16)
                e.dma_start(sud16[:], sud_d.rearrange("(c j) -> c j", j=64)).then_inc(s_ld0.h, 16)
                e.dma_start(wsdp16[:], wscd_d.rearrange("(a p) -> a p", p=128)).then_inc(s_ld0.h, 16)
                for i, b in enumerate([bq_d, bk_d, bv_d, bo_d]):
                    e.dma_start(b16[0:16, i * 128:(i + 1) * 128],
                                b.rearrange("(a p) -> a p", p=128)).then_inc(s_ld0.h, 16)
                s_ld0.n += 16 * 12

            @blk.gpsimd
            def _(e):
                if not on("0"): return
                e.dma_start(su16[:], suu_d.rearrange("(a p) -> a p", p=128)).then_inc(s_ldg.h, 16)
                s_ldg.n += 16

            @blk.tensor
            def _(e):
                if not on("0"): return
                e.wait_ge(s_ld0.h, s_ld0.n)
                e.wait_ge(s_ldg.h, s_ldg.n)
                e.transpose(ps_tr1[:], su16[:], identb[0:16, 0:16]).then_inc(s_p0b.h, 1)
                e.transpose(ps_tr0[:], wsdp16[:], identf[0:16, 0:16]).then_inc(s_p0b.h, 1)
                for i in range(4):
                    e.transpose(ps_trb[:, i * 16:(i + 1) * 16], b16[0:16, i * 128:(i + 1) * 128],
                                identf[0:16, 0:16]).then_inc(s_p0b.h, 1)
                s_p0b.n += 6

            @blk.scalar
            def _(e):
                if not on("0"): return
                e.wait_ge(s_p0b.h, 6)
                e.copy(sucol_b[:], ps_tr1[:]).then_inc(s_p0a.h, 1)
                s_p0a.n += 1
                e.activation(dsc[:], ps_tr0[:], AF.Copy, bias=0.0, scale=R8192).then_inc(s_p0a.h, 1)
                s_p0a.n += 1
                e.copy(bqc[:], ps_trb[:, 0:16]).then_inc(s_p0a.h, 1)
                e.copy(bkc[:], ps_trb[:, 16:32]).then_inc(s_p0a.h, 1)
                e.activation(bvs[:], ps_trb[:, 32:48], AF.Copy, bias=0.0, scale=SCALE).then_inc(s_p0a.h, 1)
                e.copy(boc[:], ps_trb[:, 48:64]).then_inc(s_p0a.h, 1)
                s_p0a.n += 4

            @blk.vector
            def _(e):
                if not on("0"): return
                e.wait_ge(s_p0a.h, s_p0a.n)
                e.wait_ge(s_ld0.h, s_ld0.n)
                e.wait_ge(s_ldg.h, s_ldg.n)
                s_p0a.inc(e.tensor_copy(sucol[:], sucol_b[:]))
                e.wait_ge(s_p0a.h, s_p0a.n)
                e.tensor_tensor(usc[:], usc[:], sud16[:], op=ALU.mult).then_inc(s_p0b.h, 1)
                s_p0b.n += 1
                e.wait_ge(s_p0b.h, s_p0b.n)
                e.tensor_scalar(usc[:], usc[:], R2048, None, op0=ALU.mult).then_inc(s_p0b.h, 1)
                s_p0b.n += 1
                e.tensor_scalar(gsc[:], gsc[:], R2048, None, op0=ALU.mult).then_inc(s_p0b.h, 1)
                s_p0b.n += 1
                e.tensor_scalar(nbq[:], bqc[:], -1.0, None, op0=ALU.mult).then_inc(s_p0b.h, 1)
                e.tensor_scalar(nbk[:], bkc[:], -1.0, None, op0=ALU.mult).then_inc(s_p0b.h, 1)
                s_p0b.n += 2
                e.wait_ge(s_p0b.h, s_p0b.n)
                for j in range(NDT):
                    v = e.tensor_scalar(h128su[:, j * 128:(j + 1) * 128], h128b[:],
                                        sucol[:, j:j + 1], None, op0=ALU.mult)
                s_p0b.inc(v)

        with nc.sbuf_tensor("xnT", [128, NDT * T], BF16) as xnT:
            xnT3 = xnT[:].rearrange("p (d t) -> p d t", d=NDT)

            with ExitStack() as pa:
                psb = lambda *a: pa.enter_context(nc.sbuf_tensor(*a))
                pps = lambda *a: pa.enter_context(nc.psum_tensor(*a))
                xt = [psb(f"xt{i}", [128, D], F32) for i in range(4)]
                stats = [psb(f"stats{i}", [128, 24], F32) for i in range(2)]
                aggr = [psb(f"aggr{i}", [128, 2], F32) for i in range(2)]
                vpe = [psb(f"vpe{i}", [128, 1], F32) for i in range(2)]
                sdv = [psb(f"sdv{i}", [128, 1], F32) for i in range(2)]
                rstd = [psb(f"rstd{i}", [128, 1], F32) for i in range(2)]
                nmean = [psb(f"nmean{i}", [128, 1], F32) for i in range(2)]
                xnnat = [psb(f"xnnat{i}", [128, D], BF16) for i in range(2)]
                ps_tr = [pps(f"ps_atr{i}", [128, 1024], BF16) for i in range(2)]
                blk = pa.enter_context(nc.Block("pa"))

                @blk.sync
                def _(e):
                    if not on("A"): return
                    for tt in range(NDT):
                        if tt >= 4:
                            e.wait_ge(s_xn.h, tt - 3)
                        e.dma_start(xt[tt % 4][:], x_d[tt * 128:(tt + 1) * 128, :]).then_inc(s_ldx[tt % 4].h, 16)
                        s_ldx[tt % 4].n += 16

                @blk.vector
                def _(e):
                    if not on("A"): return

                    def stage1(tt):
                        p = tt % 2
                        e.wait_ge(s_ldx[tt % 4].h, 16 * (tt // 4 + 1))
                        for i in range(4):
                            v = e.bn_stats(stats[p][:, i * 6:(i + 1) * 6], xt[tt % 4][:, i * 512:(i + 1) * 512])
                        s_stats.inc(v)
                        e.wait_ge(s_stats.h, s_stats.n)
                        s_aggr.inc(e.bn_aggr(aggr[p][:], stats[p][:]))
                        e.wait_ge(s_aggr.h, s_aggr.n)
                        s_vpe.inc(e.tensor_scalar(vpe[p][:], aggr[p][:, 1:2], 1.0, 1e-5, op0=ALU.mult, op1=ALU.add))

                    def stage2(tt):
                        p = tt % 2
                        e.wait_ge(s_sd.h, tt + 1)
                        s_rs.inc(e.reciprocal(rstd[p][:], sdv[p][:]))
                        e.wait_ge(s_rs.h, s_rs.n)
                        s_nm.inc(e.tensor_scalar(nmean[p][:], aggr[p][:, 0:1], rstd[p][:], None, op0=ALU.mult))
                        e.wait_ge(s_nm.h, s_nm.n)
                        if tt >= 2:
                            e.wait_ge(s_trF.h, 2 * (tt - 1))
                        s_xn.inc(e.tensor_scalar(xnnat[p][:], xt[tt % 4][:], rstd[p][:], nmean[p][:],
                                                 op0=ALU.mult, op1=ALU.subtract))

                    stage1(0)
                    for tt in range(NDT):
                        if tt + 1 < NDT:
                            stage1(tt + 1)
                        stage2(tt)

                @blk.scalar
                def _(e):
                    if not on("A"): return
                    for tt in range(NDT):
                        e.wait_ge(s_vpe.h, tt + 1)
                        if tt >= 2:
                            e.wait_ge(s_rs.h, tt - 1)
                        s_sd.inc(e.activation(sdv[tt % 2][:], vpe[tt % 2][:], AF.Sqrt, bias=0.0, scale=1.0))
                        for f in range(2):
                            e.wait_ge(s_trF.h, 2 * tt + f + 1)
                            dst = xnT3[:, f * 8:(f + 1) * 8, tt * 128:(tt + 1) * 128]
                            s_trE.inc(e.copy(dst, ps_tr[f][:]))

                @blk.tensor
                def _(e):
                    if not on("A"): return
                    for tt in range(NDT):
                        e.wait_ge(s_xn.h, tt + 1)
                        for f in range(2):
                            if tt >= 1:
                                e.wait_ge(s_trE.h, 2 * tt + f - 1)
                            for d in range(8):
                                dt = f * 8 + d
                                ins = e.transpose(ps_tr[f][:, d * 128:(d + 1) * 128],
                                                  xnnat[tt % 2][:, dt * 128:(dt + 1) * 128], identb[:])
                            s_trF.inc(ins)

            with ExitStack() as pb:
                psb = lambda *a: pb.enter_context(nc.sbuf_tensor(*a))
                pps = lambda *a: pb.enter_context(nc.psum_tensor(*a))
                ones_scan = psb("ones_scan", [128, T], BF16)
                wnat = [[psb(f"wnat{w}{i}", [128, D], BF16) for i in range(2)] for w in range(3)]
                wT = [[psb(f"wT{w}{i}", [128, D], BF16) for i in range(2)] for w in range(3)]
                phiq = psb("phiq", [128, T], F32)
                pv = psb("pv", [128, T], F32)
                kv = psb("kv", [128, T], F32)
                tq_t = psb("tq_t", [128, TC], F32)
                tk_t = psb("tk_t", [128, TC], F32)
                phik_t = psb("phik_t", [128, TC], F32)
                z0a = psb("z0a", [128, NDT * 128], F32)
                z0b = psb("z0b", [128, NDT * 128], F32)
                zst = [psb(f"zst{i}", [128, NDT * 128], BF16) for i in range(2)]
                a_q = [psb(f"a_q{i}", [128, TC], F32) for i in range(2)]
                b_q = [psb(f"b_q{i}", [128, TC], F32) for i in range(2)]
                a_k = [psb(f"a_k{i}", [128, TC], F32) for i in range(2)]
                b_k = [psb(f"b_k{i}", [128, TC], F32) for i in range(2)]
                v_s = [psb(f"v_s{i}", [128, TC], F32) for i in range(2)]
                ctxst = [psb(f"ctxst{i}", [128, T], BF16) for i in range(2)]
                ps_q = [pps(f"ps_q{i}", [128, TC], F32) for i in range(2)]
                ps_k = [pps(f"ps_k{i}", [128, TC], F32) for i in range(2)]
                ps_v = [pps(f"ps_v{i}", [128, TC], F32) for i in range(2)]
                ps_wtr = [pps(f"ps_wtr{i}", [128, 1024], BF16) for i in range(2)]
                ps_z = ps_v
                blk = pb.enter_context(nc.Block("pb"))
                NA = NDT
                wq_r = [wq_d, wk_d, wv_d]

                @blk.gpsimd
                def _(e):
                    if not on("B"): return
                    s_p0a.inc(e.memset(ones_scan[:], 1.0))
                    for a in range(NA):
                        if a >= 2:
                            e.wait_ge(s_wtrF.h, 6 * (a - 1))
                        for w in range(3):
                            e.dma_start(wnat[w][a % 2][:], wq_r[w][a * 128:(a + 1) * 128, :]).then_inc(s_wsl[a % 2].h, 16)
                            s_wsl[a % 2].n += 16

                @blk.tensor
                def _(e):
                    if not on("B"): return

                    def bfills(aa, w):
                        if w == 0:
                            dwait(e, s_wsl, aa, 3)
                        for f in range(2):
                            fill = 6 * aa + w * 2 + f
                            if fill >= 2:
                                e.wait_ge(s_wtrE.h, fill - 1)
                            for d in range(8):
                                dt = f * 8 + d
                                ins = e.transpose(ps_wtr[fill % 2][:, d * 128:(d + 1) * 128],
                                                  wnat[w][aa % 2][:, dt * 128:(dt + 1) * 128], identb[:])
                            s_wtrF.inc(ins)

                    for w in range(3):
                        bfills(0, w)
                    for a in range(NA):
                        e.wait_ge(s_wtrE.h, 6 * (a + 1))
                        for c in range(NTC):
                            k_i = a * NTC + c
                            for w, (ps_w, s_mm) in enumerate([(ps_q, s_mmq), (ps_k, s_mmk), (ps_v, s_mmv)]):
                                if k_i >= 2:
                                    if w == 0:
                                        e.wait_ge(s_aq.h, k_i - 1)
                                        e.wait_ge(s_tq.h, k_i - 1)
                                    elif w == 1:
                                        e.wait_ge(s_ak.h, k_i - 1)
                                        e.wait_ge(s_tk.h, k_i - 1)
                                    else:
                                        e.wait_ge(s_vs.h, k_i - 1)
                                        if c < 2:
                                            e.wait_ge(s_zev.h, 4 * a + c - 1)
                                for d in range(NDT):
                                    ins = e.matmul(ps_w[c % 2][:], wT[w][a % 2][:, d * 128:(d + 1) * 128],
                                                   xnT3[:, d, c * TC:(c + 1) * TC],
                                                   start=(d == 0), stop=(d == NDT - 1))
                                s_mm.inc(ins)
                            if a + 1 < NA and c < 3:
                                bfills(a + 1, c)
                        e.wait_ge(s_vs.h, 4 * (a + 1))
                        for g in range(4):
                            zidx = a * 4 + g
                            if zidx >= 2:
                                e.wait_ge(s_zev.h, zidx - 1)
                            for d in range(4):
                                dt = g * 4 + d
                                ins = e.matmul(ps_z[zidx % 2][:, d * 128:(d + 1) * 128],
                                               h128su[:, dt * 128:(dt + 1) * 128],
                                               xnT3[:, dt, a * 128:(a + 1) * 128],
                                               start=True, stop=True)
                            s_zmm.inc(ins)

                @blk.scalar
                def _(e):
                    if not on("B"): return

                    def bevacs(aa, w):
                        for f in range(2):
                            fill = 6 * aa + w * 2 + f
                            e.wait_ge(s_wtrF.h, fill + 1)
                            s_wtrE.inc(e.copy(wT[w][aa % 2][:, f * 1024:(f + 1) * 1024], ps_wtr[fill % 2][:]))

                    for w in range(3):
                        bevacs(0, w)
                    for a in range(NA):
                        for c in range(NTC):
                            k_i = a * NTC + c
                            p = c % 2
                            e.wait_ge(s_mmq.h, k_i + 1)
                            if k_i >= 2:
                                e.wait_ge(s_tq.h, k_i - 1)
                            s_aq.inc(e.activation(a_q[p][:], ps_q[p][:], AF.Relu, bias=nbq[:, a:a + 1], scale=-1.0))
                            e.wait_ge(s_aq.h, s_aq.n)
                            if k_i >= 2:
                                e.wait_ge(s_phiq.h, k_i - 1)
                            s_bq.inc(e.activation(b_q[p][:], a_q[p][:], AF.Exp, bias=0.0, scale=-1.0))
                            e.wait_ge(s_mmk.h, k_i + 1)
                            if k_i >= 2:
                                e.wait_ge(s_tk.h, k_i - 1)
                            s_ak.inc(e.activation(a_k[p][:], ps_k[p][:], AF.Relu, bias=nbk[:, a:a + 1], scale=-1.0))
                            e.wait_ge(s_ak.h, s_ak.n)
                            if k_i >= 2:
                                e.wait_ge(s_phik.h, k_i - 1)
                            s_bk.inc(e.activation(b_k[p][:], a_k[p][:], AF.Exp, bias=0.0, scale=-1.0))
                            if a + 1 < NA and c < 3:
                                bevacs(a + 1, c)
                        for g in range(4):
                            zidx = a * 4 + g
                            e.wait_ge(s_zmm.h, zidx + 1)
                            if g == 0 and a >= 1:
                                e.wait_ge(s_st3.h, a)
                            s_zev.inc(e.copy(z0a[:, g * 512:(g + 1) * 512], ps_z[zidx % 2][:]))


                @blk.vector
                def _(e):
                    if not on("B"): return
                    e.wait_ge(s_p0a.h, s_p0a.n)
                    for a in range(NA):
                        for c in range(NTC):
                            k_i = a * NTC + c
                            p = c % 2
                            cs = slice(c * TC, (c + 1) * TC)
                            e.wait_ge(s_mmq.h, k_i + 1)
                            e.wait_ge(s_aq.h, k_i + 1)
                            s_tq.inc(e.scalar_tensor_tensor(tq_t[:], ps_q[p][:], bqc[:, a:a + 1],
                                                            a_q[p][:], op0=ALU.add, op1=ALU.add))
                            e.wait_ge(s_tq.h, s_tq.n)
                            e.wait_ge(s_bq.h, k_i + 1)
                            s_phiq.inc(e.tensor_tensor(phiq[:, cs], tq_t[:], b_q[p][:], op=ALU.add))
                            e.wait_ge(s_mmk.h, k_i + 1)
                            e.wait_ge(s_ak.h, k_i + 1)
                            s_tk.inc(e.scalar_tensor_tensor(tk_t[:], ps_k[p][:], bkc[:, a:a + 1],
                                                            a_k[p][:], op0=ALU.add, op1=ALU.add))
                            e.wait_ge(s_tk.h, s_tk.n)
                            e.wait_ge(s_bk.h, k_i + 1)
                            s_phik.inc(e.tensor_tensor(phik_t[:], tk_t[:], b_k[p][:], op=ALU.add))
                            e.wait_ge(s_mmv.h, k_i + 1)
                            s_vs.inc(e.tensor_scalar(v_s[p][:], ps_v[p][:], SCALE, bvs[:, a:a + 1],
                                                     op0=ALU.mult, op1=ALU.add))
                            e.wait_ge(s_phik.h, s_phik.n)
                            e.wait_ge(s_vs.h, s_vs.n)
                            s_pv.inc(e.tensor_tensor(pv[:, cs], phik_t[:], v_s[p][:], op=ALU.mult))
                        e.wait_ge(s_pv.h, s_pv.n)
                        s_kv.inc(e.tensor_tensor_scan(kv[:], ones_scan[:], pv[:], 0.0,
                                                      op0=ALU.mult, op1=ALU.add))
                        e.wait_ge(s_kv.h, s_kv.n)
                        e.wait_ge(s_phiq.h, s_phiq.n)
                        if a >= 2:
                            dwait(e, s_ctxd, a - 2, 1)
                        s_ctx.inc(e.tensor_tensor(ctxst[a % 2][:], phiq[:], kv[:], op=ALU.mult))
                        CHB = 128
                        e.wait_ge(s_zev.h, 4 * (a + 1))
                        if a >= 1:
                            e.wait_ge(s_st4.h, a)
                        if a >= 2:
                            dwait(e, s_zd2, a - 2, 1)
                        for stage, (bsrc, bdst) in enumerate([(z0a, z0b), (z0b, z0a), (z0a, z0b)]):
                            m = [8, 4, 2][stage]
                            for dt in range(NDT):
                                i0, i1 = dt & ~m, dt | m
                                op = ALU.add if (dt & m) == 0 else ALU.subtract
                                v = e.tensor_tensor(bdst[:, dt * CHB:(dt + 1) * CHB],
                                                    bsrc[:, i0 * CHB:(i0 + 1) * CHB],
                                                    bsrc[:, i1 * CHB:(i1 + 1) * CHB], op=op)
                            [s_st1, s_st2, s_st3][stage].inc(v)
                            e.wait_ge([s_st1, s_st2, s_st3][stage].h, [s_st1, s_st2, s_st3][stage].n)
                        for dt in range(NDT):
                            i0, i1 = dt & ~1, dt | 1
                            op = ALU.add if (dt & 1) == 0 else ALU.subtract
                            v = e.tensor_tensor(zst[a % 2][:, dt * CHB:(dt + 1) * CHB],
                                                z0b[:, i0 * CHB:(i0 + 1) * CHB],
                                                z0b[:, i1 * CHB:(i1 + 1) * CHB], op=op)
                        s_st4.inc(v)

                @blk.sync
                def _(e):
                    if not on("B"): return
                    z2v = z2_d.rearrange("p (d t) -> p d t", d=NDT)
                    for a in range(NA):
                        e.wait_ge(s_ctx.h, a + 1)
                        e.dma_start(ctx2_d[a * 128:(a + 1) * 128, :], ctxst[a % 2][:]).then_inc(s_ctxd[a % 2].h, 16)
                        s_ctxd[a % 2].n += 16
                        e.wait_ge(s_st4.h, a + 1)
                        e.dma_start(z2v[:, :, a * 128:(a + 1) * 128], zst[a % 2][:]).then_inc(s_zd2[a % 2].h, 16)
                        s_zd2[a % 2].n += 16
                    e.wait_ge(s_ctxd[0].h, s_ctxd[0].n)
                    e.wait_ge(s_ctxd[1].h, s_ctxd[1].n)
                    e.wait_ge(s_zd2[0].h, s_zd2[0].n)
                    e.wait_ge(s_zd2[1].h, s_zd2[1].n)


            with ExitStack() as pc:
                psb = lambda *a: pc.enter_context(nc.sbuf_tensor(*a))
                pps = lambda *a: pc.enter_context(nc.psum_tensor(*a))
                wonat = [psb(f"wonat{i}", [128, D], BF16) for i in range(2)]
                woT = psb("woT", [128, NDT * D], BF16)
                ctxsl = [psb(f"ctxsl{i}", [128, NDT * TC], BF16) for i in range(2)]
                sst = [psb(f"sst{i}", [128, TC], BF16) for i in range(8)]
                ps_wtr = [pps(f"ps_cwtr{i}", [128, 1024], BF16) for i in range(4)]
                ps_s = [pps(f"ps_s{i}", [128, TC], F32) for i in range(4)]
                blk = pc.enter_context(nc.Block("pc"))
                woT3 = woT[:].rearrange("p (a m q) -> p a m q", a=NDT, m=NDT)
                ctxsl3 = [ctxsl[i].rearrange("p (a t) -> p a t", a=NDT) for i in range(2)]

                @blk.gpsimd
                def _(e):
                    if not on("C"): return
                    for m in range(NDT):
                        if m >= 2:
                            e.wait_ge(s_wotrF.h, 2 * (m - 1))
                        e.dma_start(wonat[m % 2][:], wo_d[m * 128:(m + 1) * 128, :]).then_inc(s_wosl[m % 2].h, 16)
                        s_wosl[m % 2].n += 16
                    for c in range(NTC):
                        for et in range(NDT):
                            e.wait_ge(s_sev.h, c * NDT + et + 1)
                            kk = c * NDT + et
                            e.dma_start(s2d_d[et * 128:(et + 1) * 128, c * TC:(c + 1) * TC],
                                        sst[kk % 8][:]).then_inc(s_sd2[kk % 2].h, 16)
                            s_sd2[kk % 2].n += 16
                    e.wait_ge(s_sd2[0].h, s_sd2[0].n)
                    e.wait_ge(s_sd2[1].h, s_sd2[1].n)

                @blk.sync
                def _(e):
                    if not on("C"): return
                    e.wait_ge(s_ctxd[0].h, s_ctxd[0].n)
                    e.wait_ge(s_ctxd[1].h, s_ctxd[1].n)
                    src = ctx2_d.rearrange("(a p) t -> p a t", p=128)
                    for c in range(NTC):
                        if c >= 2:
                            e.wait_ge(s_mmo.h, 16 * (c - 1))
                        for half in range(4):
                            ins = e.dma_start(ctxsl3[c % 2][:, half * 4:(half + 1) * 4, :],
                                              src[:, half * 4:(half + 1) * 4, c * TC:(c + 1) * TC])
                            ins.then_inc(s_ctxsl[c % 2].h, 16)
                            s_ctxsl[c % 2].n += 16

                @blk.tensor
                def _(e):
                    if not on("C"): return
                    for m in range(NDT):
                        dwait(e, s_wosl, m, 1)
                        for f in range(2):
                            fill = 2 * m + f
                            if fill >= 4:
                                e.wait_ge(s_wotrE.h, fill - 3)
                            for d in range(8):
                                a = f * 8 + d
                                ins = e.transpose(ps_wtr[fill % 4][:, d * 128:(d + 1) * 128],
                                                  wonat[m % 2][:, a * 128:(a + 1) * 128], identb[:])
                            s_wotrF.inc(ins)
                    for c in range(NTC):
                        dwait(e, s_ctxsl, c, 4)
                        for et in range(NDT):
                            k_i = c * NDT + et
                            bank = et % 4
                            e.wait_ge(s_wotrE.h, 2 * (et + 1))
                            if k_i >= 4:
                                e.wait_ge(s_sev.h, k_i - 3)
                            for a in range(NDT):
                                ins = e.matmul(ps_s[bank][:], woT3[:, a, et, :],
                                               ctxsl3[c % 2][:, a, :], start=(a == 0), stop=(a == NDT - 1))
                            s_mmo.inc(ins)

                @blk.scalar
                def _(e):
                    if not on("C"): return
                    for m in range(NDT):
                        for f in range(2):
                            fill = 2 * m + f
                            e.wait_ge(s_wotrF.h, fill + 1)
                            dst = woT3[:, f * 8:(f + 1) * 8, m, :]
                            s_wotrE.inc(e.copy(dst, ps_wtr[fill % 4][:]))

                @blk.vector
                def _(e):
                    if not on("C"): return
                    for c in range(NTC):
                        for et in range(NDT):
                            k_i = c * NDT + et
                            e.wait_ge(s_mmo.h, k_i + 1)
                            if k_i >= 8:
                                dwait(e, s_sd2, k_i - 2, 1)
                            s_sev.inc(e.tensor_scalar(sst[k_i % 8][:], ps_s[et % 4][:], 1.0,
                                                      boc[:, et:et + 1], op0=ALU.mult, op1=ALU.add))

            with nc.sbuf_tensor("z", [128, NDT * T], BF16) as z:
                z3 = z[:].rearrange("p (d t) -> p d t", d=NDT)

                if debug:
                    with ExitStack() as pdbg:
                        blk = pdbg.enter_context(nc.Block("pdbg"))
                        sdbg = Ctr(pdbg, nc, "sdbg")

                        @blk.sync
                        def _(e):
                            if not on("Z"): return
                            e.dma_start(dbg_xnT_d, xnT[:]).then_inc(sdbg.h, 16)
                            e.dma_start(dbg_z_d, z[:]).then_inc(sdbg.h, 16)
                            e.wait_ge(sdbg.h, 32)

                with ExitStack() as pe:
                    psb = lambda *a: pe.enter_context(nc.sbuf_tensor(*a))
                    pps = lambda *a: pe.enter_context(nc.psum_tensor(*a))
                    wupn = [psb(f"wupn{i}", [128, D], BF16) for i in range(2)]
                    wgn = [psb(f"wgn{i}", [128, D], BF16) for i in range(2)]
                    wupT = [psb(f"wupT{i}", [128, D], BF16) for i in range(2)]
                    wgT = [psb(f"wgT{i}", [128, D], BF16) for i in range(2)]
                    u_t = [psb(f"u_t{i}", [128, T], BF16) for i in range(2)]
                    ystage = [psb(f"ystage{i}", [128, T], BF16) for i in range(2)]
                    sg_t = [psb(f"sg_t{i}", [128, TC], F32) for i in range(2)]
                    ps_up = [pps(f"ps_up{i}", [128, TC], F32) for i in range(2)]
                    ps_g = [pps(f"ps_g{i}", [128, TC], F32) for i in range(2)]
                    ps_y = [pps(f"ps_y{i}", [128, TC], F32) for i in range(2)]
                    ps_wtr = [pps(f"ps_ewtr{i}", [128, 1024], BF16) for i in range(2)]
                    blk = pe.enter_context(nc.Block("pe"))
                    wup_r = wup_d.rearrange("(c j) d -> j c d", j=64)
                    wg_r = wg_d.rearrange("(c j) d -> j c d", j=64)
                    y2_r = y2_d.rearrange("(c j) t -> j c t", j=64)

                    @blk.gpsimd
                    def _(e):
                        if not on("E"): return
                        for j in range(64):
                            if j >= 2:
                                e.wait_ge(s_ewtrF.h, 4 * (j - 1))
                            e.dma_start(wupn[j % 2][:], wup_r[j]).then_inc(s_ewsl[j % 2].h, 16)
                            e.dma_start(wgn[j % 2][:], wg_r[j]).then_inc(s_ewsl[j % 2].h, 16)
                            s_ewsl[j % 2].n += 32

                    @blk.tensor
                    def _(e):
                        if not on("E"): return
                        pend = []

                        def emit_h1(jj, cc):
                            kk = 4 * jj + cc
                            e.wait_ge(s_eu.h, kk + 1)
                            if kk >= 2:
                                e.wait_ge(s_eyev.h, kk - 1)
                            ins = e.matmul(ps_y[cc % 2][:], h128b[:],
                                           u_t[jj % 2][:, cc * TC:(cc + 1) * TC], start=True, stop=True)
                            s_emy.inc(ins)

                        def fills(jj, pair):
                            if pair == 0:
                                dwait(e, s_ewsl, jj, 2)
                            wn = wupn if pair == 0 else wgn
                            for f in range(2):
                                fill = 4 * jj + pair * 2 + f
                                if fill >= 2:
                                    e.wait_ge(s_ewtrE.h, fill - 1)
                                for d in range(8):
                                    dt = pair * 0 + f * 8 + d
                                    ins = e.transpose(ps_wtr[fill % 2][:, d * 128:(d + 1) * 128],
                                                      wn[jj % 2][:, dt * 128:(dt + 1) * 128], identb[:])
                                s_ewtrF.inc(ins)

                        def chunk_mms(j, c):
                            k_i = 4 * j + c
                            if k_i >= 2:
                                e.wait_ge(s_eu.h, k_i - 1)
                            for d in range(NDT):
                                ins = e.matmul(ps_up[c % 2][:], wupT[j % 2][:, d * 128:(d + 1) * 128],
                                               z3[:, d, c * TC:(c + 1) * TC], start=(d == 0), stop=(d == NDT - 1))
                            s_emu.inc(ins)
                            if k_i >= 2:
                                e.wait_ge(s_esg.h, k_i - 1)
                            for d in range(NDT):
                                ins = e.matmul(ps_g[c % 2][:], wgT[j % 2][:, d * 128:(d + 1) * 128],
                                               z3[:, d, c * TC:(c + 1) * TC], start=(d == 0), stop=(d == NDT - 1))
                            s_emg.inc(ins)
                            pend.append((j, c))
                            if len(pend) >= 2:
                                emit_h1(*pend.pop(0))

                        fills(0, 0)
                        fills(0, 1)
                        e.wait_ge(s_zld.h, 128)
                        for j in range(64):
                            e.wait_ge(s_ewtrE.h, 4 * (j + 1))
                            chunk_mms(j, 0)
                            chunk_mms(j, 1)
                            if j + 1 < 64:
                                fills(j + 1, 0)
                            chunk_mms(j, 2)
                            if j + 1 < 64:
                                fills(j + 1, 1)
                            chunk_mms(j, 3)
                        for jj, cc in pend:
                            emit_h1(jj, cc)

                    @blk.scalar
                    def _(e):
                        if not on("E"): return
                        pendy = []

                        def emit_yev(jj, cc):
                            kk = 4 * jj + cc
                            e.wait_ge(s_emy.h, kk + 1)
                            if jj >= 2 and cc == 0:
                                dwait(e, s_eyd, jj - 2, 1)
                            s_eyev.inc(e.copy(ystage[jj % 2][:, cc * TC:(cc + 1) * TC], ps_y[cc % 2][:]))

                        def wtevacs(jj, pair):
                            for f in range(2):
                                fill = 4 * jj + pair * 2 + f
                                e.wait_ge(s_ewtrF.h, fill + 1)
                                dstT = (wupT if pair == 0 else wgT)[jj % 2]
                                s_ewtrE.inc(e.copy(dstT[:, f * 1024:(f + 1) * 1024], ps_wtr[fill % 2][:]))

                        wtevacs(0, 0)
                        wtevacs(0, 1)
                        for j in range(64):
                            for c in range(NTC):
                                k_i = 4 * j + c
                                e.wait_ge(s_emg.h, k_i + 1)
                                if k_i >= 2:
                                    e.wait_ge(s_eu.h, k_i - 1)
                                s_esg.inc(e.activation(sg_t[c % 2][:], ps_g[c % 2][:], AF.Silu,
                                                       bias=0.0, scale=gsc[:, j:j + 1]))
                                pendy.append((j, c))
                                if len(pendy) > 2:
                                    emit_yev(*pendy.pop(0))
                                if c == 1 and j + 1 < 64:
                                    wtevacs(j + 1, 0)
                                if c == 2 and j + 1 < 64:
                                    wtevacs(j + 1, 1)
                        for jj, cc in pendy:
                            emit_yev(jj, cc)

                    @blk.vector
                    def _(e):
                        if not on("E"): return
                        for j in range(64):
                            for c in range(NTC):
                                k_i = 4 * j + c
                                e.wait_ge(s_emu.h, k_i + 1)
                                e.wait_ge(s_esg.h, k_i + 1)
                                if c == 0 and j >= 2:
                                    e.wait_ge(s_emy.h, 4 * (j - 1))
                                s_eu.inc(e.scalar_tensor_tensor(u_t[j % 2][:, c * TC:(c + 1) * TC],
                                                                ps_up[c % 2][:], usc[:, j:j + 1],
                                                                sg_t[c % 2][:], op0=ALU.mult, op1=ALU.mult))

                    @blk.sync
                    def _(e):
                        if not on("E"): return
                        e.wait_ge(s_zd2[0].h, s_zd2[0].n)
                        e.wait_ge(s_zd2[1].h, s_zd2[1].n)
                        for zc in range(8):
                            zcs = slice(zc * (NDT * T // 8), (zc + 1) * (NDT * T // 8))
                            e.dma_start(z[:, zcs], z2_d[:, zcs]).then_inc(s_zld.h, 16)
                            s_zld.n += 16
                        for j in range(64):
                            e.wait_ge(s_eyev.h, 4 * (j + 1))
                            e.dma_start(y2_r[j], ystage[j % 2][:]).then_inc(s_eyd[j % 2].h, 16)
                            s_eyd[j % 2].n += 16
                        e.wait_ge(s_eyd[0].h, s_eyd[0].n)
                        e.wait_ge(s_eyd[1].h, s_eyd[1].n)

        with ExitStack() as pf:
            wdTp = pf.enter_context(nc.sbuf_tensor("wdTp", [128, 8 * 64 * 128], BF16))
            wdTp3 = wdTp[:].rearrange("p (et a q) -> p et a q", et=8, a=64)
            for h in range(2):
                with ExitStack() as pfp:
                    psb = lambda *a: pfp.enter_context(nc.sbuf_tensor(*a))
                    pps = lambda *a: pfp.enter_context(nc.psum_tensor(*a))
                    wdn = [psb(f"wdn{h}{i}", [128, II], BF16) for i in range(2)]
                    wdraw = [psb(f"wdraw{h}{i}", [128, 1024], BF16) for i in range(8)]
                    ps_ftr = [pps(f"ps_ftr{h}{i}", [128, 1024], BF16) for i in range(2)]
                    ps_fs2 = [pps(f"ps_fs2{h}{i}", [128, 512], F32) for i in range(4)]
                    blk = pfp.enter_context(nc.Block(f"pfp{h}"))
                    base_tr = s_ftrF.n
                    s_fs2m_base = s_fs2m.n
                    s_fs2e_base = s_fs2e.n
                    base_et = 8 * h

                    @blk.gpsimd
                    def _(e):
                        if not on("F"): return
                        for et in range(8):
                            if et >= 2:
                                e.wait_ge(s_ftrF.h, base_tr + 8 * (et - 1))
                            ge = 8 * h + et
                            e.dma_start(wdn[et % 2][:],
                                        wd_d[(base_et + et) * 128:(base_et + et + 1) * 128, :]).then_inc(s_fwsl[ge % 2].h, 16)
                            s_fwsl[ge % 2].n += 16

                    @blk.tensor
                    def _(e):
                        if not on("F"): return
                        pends = []

                        def emit_s2(fill):
                            e.wait_ge(s_ftrE.h, fill + 1)
                            lf = (fill - base_tr) % 8
                            for g2 in range(2):
                                idx2 = 2 * (fill - base_tr) + g2
                                if idx2 >= 4:
                                    e.wait_ge(s_fs2e.h, s_fs2e_base + idx2 - 3)
                                for d in range(4):
                                    dd = g2 * 4 + d
                                    ins = e.matmul(ps_fs2[idx2 % 4][:, d * 128:(d + 1) * 128], s2b[:],
                                                   wdraw[lf][:, dd * 128:(dd + 1) * 128],
                                                   start=True, stop=True)
                                s_fs2m.inc(ins)

                        for et in range(8):
                            dwait(e, s_fwsl, 8 * h + et, 1)
                            for f in range(8):
                                fill = base_tr + 8 * et + f
                                if fill >= base_tr + 2:
                                    e.wait_ge(s_ftrE.h, fill - 1)
                                for d in range(8):
                                    a2 = f * 8 + d
                                    ins = e.transpose(ps_ftr[fill % 2][:, d * 128:(d + 1) * 128],
                                                      wdn[et % 2][:, a2 * 128:(a2 + 1) * 128], identb[:])
                                s_ftrF.inc(ins)
                                pends.append(fill)
                                if len(pends) > 2:
                                    emit_s2(pends.pop(0))
                        for fill in pends:
                            emit_s2(fill)

                    @blk.scalar
                    def _(e):
                        if not on("F"): return
                        pend2 = []

                        def emit_s2e(locfill):
                            et2, f2 = divmod(locfill, 8)
                            for g2 in range(2):
                                idx2 = 2 * locfill + g2
                                e.wait_ge(s_fs2m.h, s_fs2m_base + idx2 + 1)
                                dst = wdTp3[:, et2, f2 * 8 + g2 * 4:f2 * 8 + (g2 + 1) * 4, :]
                                s_fs2e.inc(e.copy(dst, ps_fs2[idx2 % 4][:]))

                        for et in range(8):
                            for f in range(8):
                                fill = base_tr + 8 * et + f
                                e.wait_ge(s_ftrF.h, fill + 1)
                                if et >= 1:
                                    e.wait_ge(s_fs2m.h, s_fs2m_base + 2 * (8 * (et - 1) + f) + 2)
                                s_ftrE.inc(e.copy(wdraw[f][:], ps_ftr[fill % 2][:]))
                                pend2.append(fill - base_tr)
                                if len(pend2) > 2:
                                    emit_s2e(pend2.pop(0))
                        for locfill in pend2:
                            emit_s2e(locfill)

                with ExitStack() as pfs:
                    psb = lambda *a: pfs.enter_context(nc.sbuf_tensor(*a))
                    pps = lambda *a: pfs.enter_context(nc.psum_tensor(*a))
                    yslab = [psb(f"yslab{h}{i}", [128, 64 * FTC], BF16) for i in range(2)]
                    yslab3 = [yslab[i].rearrange("p (a t) -> p a t", a=64) for i in range(2)]
                    gst = [psb(f"gst{h}{i}", [128, FTC], BF16) for i in range(4)]
                    ps_dn = [pps(f"ps_dn{h}{i}", [128, FTC], F32) for i in range(4)]
                    blk = pfs.enter_context(nc.Block(f"pfs{h}"))
                    mmbase = s_fmm.n
                    gbase = s_fgev.n
                    y2r = y2_d.rearrange("(a p) t -> p a t", p=128)

                    @blk.sync
                    def _(e):
                        if not on("F"): return
                        e.wait_ge(s_eyd[0].h, s_eyd[0].n)
                        e.wait_ge(s_eyd[1].h, s_eyd[1].n)
                        for ts in range(NFT):
                            if ts >= 2:
                                e.wait_ge(s_fmm.h, mmbase + 8 * (ts - 1))
                            gt = 8 * h + ts
                            for part in range(8):
                                ins = e.dma_start(yslab3[ts % 2][:, part * 8:(part + 1) * 8, :],
                                                  y2r[:, part * 8:(part + 1) * 8, ts * FTC:(ts + 1) * FTC])
                                ins.then_inc(s_fysl[gt % 2].h, 16)
                                s_fysl[gt % 2].n += 16

                    @blk.tensor
                    def _(e):
                        if not on("F"): return
                        for ts in range(NFT):
                            dwait(e, s_fysl, 8 * h + ts, 8)
                            for et in range(8):
                                idx = 8 * ts + et
                                if idx >= 4:
                                    e.wait_ge(s_fgev.h, gbase + idx - 3)
                                for a2 in range(64):
                                    ins = e.matmul(ps_dn[et % 4][:], wdTp3[:, et, a2, :],
                                                   yslab3[ts % 2][:, a2, :],
                                                   start=(a2 == 0), stop=(a2 == 63))
                                s_fmm.inc(ins)

                    @blk.scalar
                    def _(e):
                        if not on("F"): return
                        for ts in range(NFT):
                            for et in range(8):
                                idx = 8 * ts + et
                                e.wait_ge(s_fmm.h, mmbase + idx + 1)
                                if idx >= 4:
                                    dwait(e, s_fgd, 64 * h + idx - 2, 1)
                                s_fgev.inc(e.activation(gst[idx % 4][:], ps_dn[et % 4][:], AF.Silu,
                                                        bias=0.0, scale=dsc[:, 8 * h + et:8 * h + et + 1]))

                    @blk.gpsimd
                    def _(e):
                        if not on("F"): return
                        for ts in range(NFT):
                            for et in range(8):
                                idx = 8 * ts + et
                                e.wait_ge(s_fgev.h, gbase + idx + 1)
                                gg = 64 * h + idx
                                erow = (8 * h + et) * 128
                                e.dma_start(gd_d[erow:erow + 128, ts * FTC:(ts + 1) * FTC],
                                            gst[idx % 4][:]).then_inc(s_fgd[gg % 2].h, 16)
                                s_fgd[gg % 2].n += 16
                        e.wait_ge(s_fgd[0].h, s_fgd[0].n)
                        e.wait_ge(s_fgd[1].h, s_fgd[1].n)

        with ExitStack() as pg:
            psb = lambda *a: pg.enter_context(nc.sbuf_tensor(*a))
            pps = lambda *a: pg.enter_context(nc.psum_tensor(*a))
            ssl = [psb(f"ssl{i}", [128, NDT * TC], BF16) for i in range(2)]
            gsl = [psb(f"gsl{i}", [128, NDT * TC], BF16) for i in range(2)]
            rT = [psb(f"rT{i}", [128, NDT * TC], BF16) for i in range(2)]
            xt2 = [psb(f"xt2{i}", [128, D], F32) for i in range(4)]
            rnat = [psb(f"rnat{i}", [128, D], BF16) for i in range(4)]
            onat = [psb(f"onat{i}", [128, D], F32) for i in range(4)]
            ps_gtr = [pps(f"ps_gtr{i}", [128, 1024], BF16) for i in range(2)]
            blk = pg.enter_context(nc.Block("pg"))
            rT3 = [rT[i].rearrange("p (a t) -> p a t", a=NDT) for i in range(2)]
            s2r = s2d_d.rearrange("(a p) t -> p a t", p=128)
            gdr = gd_d.rearrange("(a p) t -> p a t", p=128)

            @blk.sync
            def _(e):
                if not on("G"): return
                e.wait_ge(s_sd2[0].h, s_sd2[0].n)
                e.wait_ge(s_sd2[1].h, s_sd2[1].n)
                for tg in range(NTC):
                    if tg >= 2:
                        e.wait_ge(s_grm.h, tg - 1)
                    for part in range(4):
                        e.dma_start(ssl[tg % 2].rearrange("p (a t) -> p a t", a=NDT)[:, part * 4:(part + 1) * 4, :],
                                    s2r[:, part * 4:(part + 1) * 4, tg * TC:(tg + 1) * TC]).then_inc(s_gsl[tg % 2].h, 16)
                        s_gsl[tg % 2].n += 16


            @blk.gpsimd
            def _(e):
                if not on("G"): return
                e.wait_ge(s_fgd[0].h, s_fgd[0].n)
                e.wait_ge(s_fgd[1].h, s_fgd[1].n)
                for tg in range(NTC):
                    if tg >= 2:
                        e.wait_ge(s_grm.h, tg - 1)
                    for part in range(4):
                        e.dma_start(gsl[tg % 2].rearrange("p (a t) -> p a t", a=NDT)[:, part * 4:(part + 1) * 4, :],
                                    gdr[:, part * 4:(part + 1) * 4, tg * TC:(tg + 1) * TC]).then_inc(s_gsl2[tg % 2].h, 16)
                        s_gsl2[tg % 2].n += 16

            @blk.vector
            def _(e):
                if not on("G"): return
                for tg in range(NTC + 1):
                    if tg < NTC:
                        dwait(e, s_gsl, tg, 4)
                        dwait(e, s_gsl2, tg, 4)
                        if tg >= 2:
                            e.wait_ge(s_gtrF.h, 8 * (tg - 1))
                        s_grm.inc(e.tensor_tensor(rT[tg % 2][:], ssl[tg % 2][:], gsl[tg % 2][:], op=ALU.mult))
                    if tg >= 1:
                        for q in range(4):
                            ti = (tg - 1) * 4 + q
                            e.wait_ge(s_gx[ti % 4].h, 16 * (ti // 4 + 1))
                            e.wait_ge(s_gtrE.h, 2 * (ti + 1))
                            if ti >= 4:
                                e.wait_ge(s_gout[ti % 4].h, 16 * ((ti - 4) // 4 + 1))
                            s_gadd.inc(e.tensor_tensor(onat[ti % 4][:], xt2[ti % 4][:], rnat[ti % 4][:], op=ALU.add))

            @blk.tensor
            def _(e):
                if not on("G"): return
                for tg in range(NTC):
                    e.wait_ge(s_grm.h, tg + 1)
                    for q in range(4):
                        ti = tg * 4 + q
                        for f in range(2):
                            fill = 2 * ti + f
                            if fill >= 2:
                                e.wait_ge(s_gtrE.h, fill - 1)
                            for d in range(8):
                                et = f * 8 + d
                                ins = e.transpose(ps_gtr[fill % 2][:, d * 128:(d + 1) * 128],
                                                  rT3[tg % 2][:, et, q * 128:(q + 1) * 128], identb[:])
                            s_gtrF.inc(ins)

            @blk.scalar
            def _(e):
                if not on("G"): return
                for ti in range(4):
                    e.dma_start(xt2[ti % 4][:], x_d[ti * 128:(ti + 1) * 128, :]).then_inc(s_gx[ti % 4].h, 16)
                    s_gx[ti % 4].n += 16
                for ti in range(NDT):
                    for f in range(2):
                        fill = 2 * ti + f
                        e.wait_ge(s_gtrF.h, fill + 1)
                        if ti >= 4 and f == 0:
                            e.wait_ge(s_gadd.h, ti - 3)
                        s_gtrE.inc(e.copy(rnat[ti % 4][:, f * 1024:(f + 1) * 1024], ps_gtr[fill % 2][:]))
                    if ti + 4 < NDT:
                        tj = ti + 4
                        e.wait_ge(s_gadd.h, tj - 3)
                        e.dma_start(xt2[tj % 4][:], x_d[tj * 128:(tj + 1) * 128, :]).then_inc(s_gx[tj % 4].h, 16)
                        s_gx[tj % 4].n += 16

            @blk.gpsimd
            def _(e):
                if not on("G"): return
                for ti in range(NDT):
                    e.wait_ge(s_gadd.h, ti + 1)
                    e.dma_start(out_d[ti * 128:(ti + 1) * 128, :], onat[ti % 4][:]).then_inc(s_gout[ti % 4].h, 16)
                    s_gout[ti % 4].n += 16
                for p in range(4):
                    e.wait_ge(s_gout[p].h, s_gout[p].n)

    return nc


_H128 = _had(128).astype(BF)
_S2C = np.kron(np.eye(2, dtype=np.float32), _had(64)).astype(BF)
_IDENT = np.eye(128, dtype=np.float32)


def kernel(L0_attn_x, L0_attn_Wq, L0_attn_bq, L0_attn_Wk, L0_attn_bk, L0_attn_Wv,
           L0_attn_bv, L0_attn_Wo, L0_attn_bo, L0_ffn_SU_ug, L0_ffn_SU_dp,
           L0_ffn_Wscale_ug, L0_ffn_Wscale_dp, L0_ffn_W_up, L0_ffn_W_gate, L0_ffn_W_down):
    nc = build_program(debug=False)
    x = np.ascontiguousarray(np.asarray(L0_attn_x, dtype=np.float32))
    shared = {
        "wq": np.ascontiguousarray(L0_attn_Wq, dtype=np.float32),
        "wk": np.ascontiguousarray(L0_attn_Wk, dtype=np.float32),
        "wv": np.ascontiguousarray(L0_attn_Wv, dtype=np.float32),
        "wo": np.ascontiguousarray(L0_attn_Wo, dtype=np.float32),
        "bq": np.ascontiguousarray(L0_attn_bq, dtype=np.float32),
        "bk": np.ascontiguousarray(L0_attn_bk, dtype=np.float32),
        "bv": np.ascontiguousarray(L0_attn_bv, dtype=np.float32),
        "bo": np.ascontiguousarray(L0_attn_bo, dtype=np.float32),
        "su_ug": np.ascontiguousarray(L0_ffn_SU_ug, dtype=np.float32),
        "su_dp": np.ascontiguousarray(L0_ffn_SU_dp, dtype=np.float32),
        "wsc_ug": np.ascontiguousarray(L0_ffn_Wscale_ug, dtype=np.float32),
        "wsc_dp": np.ascontiguousarray(L0_ffn_Wscale_dp, dtype=np.float32),
        "wup": np.ascontiguousarray(L0_ffn_W_up, dtype=np.float32),
        "wg": np.ascontiguousarray(L0_ffn_W_gate, dtype=np.float32),
        "wd": np.ascontiguousarray(L0_ffn_W_down, dtype=np.float32),
        "h128": _H128, "s2c": _S2C,
        "identc": _IDENT.astype(BF), "identf": _IDENT,
    }
    in_maps = [dict(shared, x=np.ascontiguousarray(x[c])) for c in range(8)]
    res = run_bass_kernel_spmd(nc, in_maps, core_ids=list(range(8)))
    return np.stack([res.results[c]["out"] for c in range(8)], axis=0)
